# version 12
# baseline (speedup 1.0000x reference)
import numpy as np
from contextlib import ExitStack

import concourse.bass as bass
import concourse.mybir as mybir
from concourse.bass_utils import run_bass_kernel_spmd

F32 = mybir.dt.float32
BF16 = mybir.dt.bfloat16
AF = mybir.ActivationFunctionType
ALU = mybir.AluOpType
AX = mybir.AxisListType.X

ENGINES = ("sync", "gpsimd", "tensor", "vector", "scalar")

B, N, D, H, HD = 2, 16384, 2048, 16, 128
I = 8192
KC = D // 128
NB = I // 128
EPS = 1e-5
CH = 128


class Cfg:
    def __init__(self, slice_tokens=4096, look_tokens=4096, st=512):
        self.SLICE = slice_tokens
        self.LOOK = look_tokens
        self.ST = st
        self.NCH = st // CH
        self.NST = slice_tokens // st
        self.NLST = look_tokens // st
        self.WIN = self.LOOK + self.SLICE


class Prog:
    def __init__(self, nc, block, ctx, debug=False):
        self.nc, self.block = nc, block
        self.ctx = ctx
        self.debug = debug
        self.handles = {}
        self.counts = {}
        for en in ENGINES:
            self.sem("c_" + en)
        self.marks = {en: {} for en in ENGINES}
        self.steps = []
        self.barrier_toks = []
        self.phase_first = {en: True for en in ENGINES}

    def sem(self, name):
        if name not in self.handles:
            self.handles[name] = self.ctx.enter_context(self.nc.semaphore(name))
            self.counts[name] = 0
        return name

    def step(self, engine, fn, deps=(), dma=False, n_dma=1, exempt=False,
             dma_sem=None):
        if dma:
            assert dma_sem is not None, "dma steps need a dedicated semaphore"
            sem = self.sem(dma_sem)
        else:
            sem = "c_" + engine
        self.counts[sem] += 16 * n_dma if dma else 1
        flat = []
        for d in deps:
            if d is None:
                continue
            if isinstance(d, list):
                flat.extend(x for x in d if x is not None)
            else:
                flat.append(d)
        if self.phase_first[engine] and not exempt:
            flat = flat + self.barrier_toks
            self.phase_first[engine] = False
        self.steps.append((engine, sem, self.counts[sem], flat, fn, dma, n_dma, exempt))
        return (sem, self.counts[sem])

    def snapshot(self):
        return [("c_" + en, self.counts["c_" + en])
                for en in ("tensor", "vector", "scalar")
                if self.counts["c_" + en] > 0]

    def new_phase(self):
        toks = [("c_" + en, self.counts["c_" + en])
                for en in ("tensor", "vector", "scalar", "gpsimd")]
        self.barrier_toks = [t for t in toks if t[1] > 0]
        if self.debug and "d_dbg" in self.counts:
            self.barrier_toks.append(("d_dbg", self.counts["d_dbg"]))
        self.phase_first = {en: True for en in ENGINES}

    def flush(self):
        steps = self.steps
        self.steps = []
        for en in ENGINES:
            sub = [s for s in steps if s[0] == en]
            if not sub:
                continue

            def body(eng, sub=sub, en=en):
                marks = self.marks[en]
                for (_, sem, val, deps, fn, dma, n_dma, exempt) in sub:
                    for (dsem, dval) in deps:
                        if dval <= 0:
                            continue
                        if marks.get(dsem, 0) >= dval:
                            continue
                        eng.wait_ge(self.handles[dsem], dval)
                        marks[dsem] = dval
                    insts = fn(eng)
                    if insts is None:
                        raise RuntimeError("step fn returned no instruction")
                    if not isinstance(insts, (list, tuple)):
                        insts = [insts]
                    if dma:
                        assert len(insts) == n_dma, (len(insts), n_dma)
                        for inst in insts:
                            inst.then_inc(self.handles[sem], 16)
                    else:
                        insts[-1].then_inc(self.handles[sem], 1)

            getattr(self.block, en)(body)


class Ring:

    def __init__(self, P, slot_aps, name):
        self.P = P
        self.slots = slot_aps
        self.name = name
        self.free = [[] for _ in slot_aps]
        self.i = 0

    def load(self, src_ap, dep=None):
        s = self.i % len(self.slots)
        self.i += 1
        slot = self.slots[s]
        deps = list(self.free[s])
        if dep is not None:
            deps.append(dep)
        tok = self.P.step(
            "sync",
            lambda eng, slot=slot, src=src_ap: eng.dma_start(out=slot, in_=src),
            deps=deps, dma=True, exempt=True, dma_sem=f"d_{self.name}{s}")
        self.free[s] = []
        return s, slot, tok

    def consumed(self, s, tok):
        self.free[s].append(tok)


def build_core_program(cfg, debug_names=()):
    nc = bass.Bass("TRN2", target_bir_lowering=False)
    ST, NCH, NST, NLST = cfg.ST, cfg.NCH, cfg.NST, cfg.NLST

    din = {}
    def dt_in(name, shape):
        din[name] = nc.dram_tensor(name, list(shape), F32, kind="ExternalInput").ap()
        return din[name]

    xw = dt_in("xw", [cfg.WIN, D])
    wq_b = dt_in("wq_b", [KC, 128, D])
    wk_b = dt_in("wk_b", [KC, 128, D])
    wv_b = dt_in("wv_b", [KC, 128, D])
    wo_b = dt_in("wo_b", [KC, 128, D])
    wg1_in = dt_in("wg1_in", [KC, 128, HD])
    wg2_in = dt_in("wg2_in", [128, D])
    wgb_in = dt_in("wgb_in", [NB, 128, D])
    wub_in = dt_in("wub_in", [NB, 128, D])
    wdb_in = dt_in("wdb_in", [NB, 128, D])
    normw_in = dt_in("normw_in", [128, KC])
    ld_in = dt_in("ld_in", [128, H])
    tmask_in = dt_in("tmask_in", [128, 128])
    cplus1_in = dt_in("cplus1_in", [128, 1])
    c127_in = dt_in("c127_in", [128, 1])
    ones_in = dt_in("ones_in", [128, 128])
    ident_in = dt_in("ident_in", [128, 128])

    yout = nc.dram_tensor("yout", [cfg.SLICE, D], F32, kind="ExternalOutput").ap()

    cache = {
        "q": nc.dram_tensor("cache_q", [KC, 128, D], BF16, kind="Internal").ap(),
        "k": nc.dram_tensor("cache_k", [KC, 128, D], BF16, kind="Internal").ap(),
        "v": nc.dram_tensor("cache_v", [KC, 128, D], BF16, kind="Internal").ap(),
        "o": nc.dram_tensor("cache_o", [KC, 128, D], BF16, kind="Internal").ap(),
        "gb": nc.dram_tensor("cache_gb", [NB, 128, D], BF16, kind="Internal").ap(),
        "ub": nc.dram_tensor("cache_ub", [NB, 128, D], BF16, kind="Internal").ap(),
        "db": nc.dram_tensor("cache_db", [NB, 128, D], BF16, kind="Internal").ap(),
    }
    cache_src = {"q": wq_b, "k": wk_b, "v": wv_b, "o": wo_b,
                 "gb": wgb_in, "ub": wub_in, "db": wdb_in}

    sp_xn = nc.dram_tensor("sp_xn", [NST, 128, KC * ST], BF16, kind="Internal").ap()
    sp_kf = nc.dram_tensor("sp_kf", [NST, 128, KC * ST], BF16, kind="Internal").ap()
    sp_ks = nc.dram_tensor("sp_ks", [NST, NCH, 128, D], BF16, kind="Internal").ap()
    sp_v = nc.dram_tensor("sp_v", [NST, NCH, 128, D], BF16, kind="Internal").ap()

    dbg = {}

    ctx = ExitStack()
    sb = lambda name, shape, dt: ctx.enter_context(nc.sbuf_tensor(name, list(shape), dt))

    xn_fm = sb("xn_fm", [128, KC, ST], BF16)
    q_fm = sb("q_fm", [128, KC, ST], BF16)
    k_fm = sb("k_fm", [128, KC, ST], BF16)
    v_fm = sb("v_fm", [128, NCH, D], BF16)
    o_tm = sb("o_tm", [128, NCH, D], BF16)
    onf = sb("onf", [128, KC, ST], BF16)
    S_m = sb("S_m", [128, D], F32)
    S_bf = sb("S_bf", [128, D], BF16)
    x_h2 = [sb(f"x_h2_{i}", [128, 1024], F32) for i in range(2)]
    out_hf = sb("out_hf", [128, 2, 1024], F32)
    ring_slots = [sb(f"ring{i}", [128, D], BF16) for i in range(4)]
    wg1_t = sb("wg1_t", [128, KC, HD], BF16)
    wg2_t = sb("wg2_t", [128, D], BF16)
    norm_cols = sb("norm_cols", [128, KC], F32)
    maskT = sb("maskT", [128, D], BF16)
    ld_rep = sb("ld_rep", [128, H], F32)
    qd_cols = sb("qd_cols", [128, H], F32)
    kd_cols = sb("kd_cols", [128, H], F32)
    cd_cols = sb("cd_cols", [128, H], F32)
    cp1 = sb("cp1", [128, 1], F32)
    c127 = sb("c127", [128, 1], F32)
    tmask_t = sb("tmask_t", [128, 128], F32)
    ident_bf = sb("ident_bf", [128, 128], BF16)
    msk_tmp = sb("msk_tmp", [128, 128], F32)
    stat = {n: sb("st_" + n, [128, 2 * NCH], F32)
            for n in ("sum", "sq", "mean", "ex2", "m2", "var", "rstd", "nmb")}
    tm_scr2 = [sb(f"tm_scr{i}", [128, D], BF16) for i in range(2)]
    ksc2 = [sb(f"ksc2_{i}", [128, D], BF16) for i in range(2)]
    scT = [sb(f"scT{i}", [128, 512], BF16) for i in range(2)]
    o_tmp = [sb(f"o_tmp{i}", [128, 512], BF16) for i in range(2)]
    sig = [sb(f"sig{i}", [128, 512], BF16) for i in range(2)]
    g1_fm = sb("g1_fm", [128, ST], BF16)
    h_g = [sb(f"h_g{i}", [128, 512], BF16) for i in range(2)]
    ssq = [sb(f"ssq{i}", [128, 512], BF16) for i in range(4)]
    ge = {p: sb(f"ge{p[0]}{p[1]}", [128, 512], BF16)
          for p in ((0, 1), (0, 2), (0, 3), (1, 2), (1, 3), (2, 3))}
    msq = sb("msq", [128, NCH * H], F32)
    rstd_rms = sb("rstd_rms", [128, NCH * H], F32)
    eps_col = sb("eps_col", [128, 1], F32)
    x_ln = [sb(f"x_ln{i}", [128, D], F32) for i in range(2)]

    block = ctx.enter_context(nc.Block())
    P = Prog(nc, block, ctx, debug=bool(debug_names))

    cache_tok = {}
    prod = {}
    o_done = [None] * NCH
    onf_done = [None] * NCH
    apply_tok = [None] * NB
    store_toks = []
    out_free = [[] for _ in range(2)]
    x_free = [[] for _ in range(2)]
    sbf_tok = [None] * 4
    xn_done = [None] * NCH
    tm_free = [[], []]
    x_ln_free = [[], []]
    dump_guard = [None]
    pgt_free_out = [None, None]
    g1ev_out = [None]
    wo_last = [None]
    sp_last = {"xn": None, "kf": None, "ks": None, "v": None}
    sp_done = [[]]
    kf_tok = [None]
    scan_kf_last = [None]
    xn_rd = [None]
    spv_prev = [None] * 4
    spks_prev = [None] * 2
    xn_w_dep = [None]
    kf_w_dep = [None]
    ksc_free = [None, None]
    vt_free = [None, None]

    ring = Ring(P, [t[:] for t in ring_slots], "ring")
    onf_flat = onf[:].rearrange("p a b -> p (a b)")
    ring_b = Ring(P, [onf_flat[:, i * 2048:(i + 1) * 2048] for i in range(4)], "ringB")

    def og_ap(ts, c0, c1):
        flat = v_fm[:].rearrange("p a b -> p (a b)")
        return flat[:, ts * D + c0: ts * D + c1]

    def hsp(b):
        if b < 16:
            return q_fm[:, b, :]
        if b < 32:
            return k_fm[:, b - 16, :]
        if b < 48:
            return v_fm[:].rearrange("p a b -> p (a b)").rearrange(
                "p (c e) -> p c e", c=KC)[:, b - 32, :]
        return o_tm[:].rearrange("p a b -> p (a b)").rearrange(
            "p (c e) -> p c e", c=KC)[:, b - 48, :]

    def tap(name, ap, deps):
        if name not in debug_names:
            return
        shape = list(ap.shape)
        out = nc.dram_tensor("dbg_" + name, shape, F32, kind="ExternalOutput").ap()
        dbg[name] = "dbg_" + name
        P.step("scalar", lambda eng, out=out, ap=ap: eng.dma_start(out=out, in_=ap),
               deps=deps, dma=True, dma_sem="d_dbg")

    P.new_phase()
    for dst_t, src in ((ld_rep, ld_in), (cp1, cplus1_in), (c127, c127_in),
                       (tmask_t, tmask_in)):
        small_tok = P.step(
            "sync", lambda eng, d=dst_t, s=src: eng.dma_start(out=d[:], in_=s),
            dma=True, exempt=True, dma_sem="d_ini_s")
    small_loads = [small_tok]
    for dst_t, src in ((ident_bf, ident_in), (wg2_t, wg2_in)):
        cast_tok = P.step(
            "gpsimd", lambda eng, d=dst_t, s=src: eng.dma_start(out=d[:], in_=s),
            dma=True, exempt=True, dma_sem="d_ini_g")
    nc_tok = P.step("sync", lambda eng: eng.dma_start(out=norm_cols[:], in_=normw_in),
                    dma=True, exempt=True, dma_sem="d_ini_n")
    for kc in range(KC):
        cast_tok = P.step(
            "gpsimd", lambda eng, kc=kc: eng.dma_start(out=wg1_t[:, kc, :], in_=wg1_in[kc]),
            dma=True, exempt=True, dma_sem="d_ini_g")
    cast_loads = [cast_tok, cast_tok]
    wg1_tok = cast_tok
    wg2_tok = cast_tok

    t1 = P.step("vector", lambda eng: eng.tensor_scalar(
        qd_cols[:], ld_rep[:], cp1[:, 0:1], None, ALU.mult), deps=small_loads)
    t2 = P.step("vector", lambda eng: eng.tensor_scalar(
        kd_cols[:], ld_rep[:], c127[:, 0:1], None, ALU.mult), deps=small_loads)
    s1 = P.step("scalar", lambda eng: eng.activation(
        qd_cols[:], qd_cols[:], AF.Exp), deps=[t1])
    t3 = P.step("vector", lambda eng: eng.tensor_scalar(
        qd_cols[:], qd_cols[:], 0.5, None, ALU.mult), deps=[s1])
    s2 = P.step("scalar", lambda eng: eng.activation(
        kd_cols[:], kd_cols[:], AF.Exp), deps=[t2])
    s3 = P.step("scalar", lambda eng: eng.activation(
        cd_cols[:], ld_rep[:], AF.Exp, scale=float(CH)), deps=small_loads)
    mask_toks = []
    for h in range(H):
        tv = P.step("vector", lambda eng, h=h: eng.tensor_scalar(
            msk_tmp[:], tmask_t[:], ld_rep[:, h:h + 1], None, ALU.mult),
            deps=small_loads + ([mask_toks[-1]] if mask_toks else []))
        tsc = P.step("scalar", lambda eng, h=h: eng.activation(
            maskT[:, h * 128:(h + 1) * 128], msk_tmp[:], AF.Exp), deps=[tv])
        mask_toks.append(tsc)
    mhalf = P.step("vector", lambda eng: eng.tensor_scalar(
        maskT[:], maskT[:], 0.5, None, ALU.mult), deps=[mask_toks[-1]])
    init_vec = [t3, s2, s3, mhalf]
    zs = P.step("vector", lambda eng: eng.memset(S_m[:], 0.0), deps=[])
    zeps = P.step("vector", lambda eng: eng.memset(eps_col[:], EPS), deps=[])
    zb = P.step("scalar", lambda eng: eng.memzero(S_bf[:]), deps=[])
    for g4 in range(4):
        sbf_tok[g4] = zb

    worder = (["k", "v"] if cfg.NLST else []) + ["q"]
    worder += [w for w in ("k", "v") if w not in worder]
    worder += ["gb", "ub", "db", "o"]
    for wname in worder:
        nblk = NB if wname in ("gb", "ub", "db") else KC
        for b in range(nblk):
            wtok = P.step(
                "gpsimd",
                lambda eng, wn=wname, b=b: eng.dma_start(
                    out=cache[wn][b], in_=cache_src[wn][b]),
                dma=True, exempt=True, dma_sem=f"d_w_{wname}")
        for b in range(nblk):
            cache_tok[(wname, b)] = wtok

    ln_rdy = [[None] * NCH, [None] * NCH]

    def emit_ln_stats(row0, par, q="gpsimd"):
        for ts in range(NCH):
            sl = ts % 2
            c = par * NCH + ts
            lx = P.step(q, lambda eng, r=row0 + ts * CH, sl=sl: eng.dma_start(
                out=x_ln[sl][:], in_=xw[r:r + CH, :]), deps=x_ln_free[sl],
                dma=True, exempt=True, dma_sem=f"d_xl_{q}{sl}")
            x_ln_free[sl] = []
            tsum = P.step("vector", lambda eng, sl=sl, c=c: eng.reduce_sum(
                stat["sum"][:, c:c + 1], x_ln[sl][:], AX), deps=[lx])
            tsq = P.step("scalar", lambda eng, sl=sl, c=c: eng.activation(
                x_ln[sl][:], x_ln[sl][:], AF.Square,
                accum_out=stat["sq"][:, c:c + 1]), deps=[lx, tsum])
            x_ln_free[sl] = [tsq]
            a = P.step("vector", lambda eng, c=c: eng.tensor_scalar(
                stat["mean"][:, c:c + 1], stat["sum"][:, c:c + 1],
                1.0 / D, None, ALU.mult), deps=[tsum])
            b_ = P.step("vector", lambda eng, c=c: eng.tensor_scalar(
                stat["ex2"][:, c:c + 1], stat["sq"][:, c:c + 1],
                1.0 / D, None, ALU.mult), deps=[tsq])
            c_ = P.step("scalar", lambda eng, c=c: eng.activation(
                stat["m2"][:, c:c + 1], stat["mean"][:, c:c + 1],
                AF.Square), deps=[a])
            d_ = P.step("vector", lambda eng, c=c: eng.tensor_tensor(
                stat["var"][:, c:c + 1], stat["ex2"][:, c:c + 1],
                stat["m2"][:, c:c + 1], ALU.subtract), deps=[b_, c_])
            e_ = P.step("scalar", lambda eng, c=c: eng.activation(
                stat["rstd"][:, c:c + 1], stat["var"][:, c:c + 1],
                AF.Sqrt, bias=eps_col[:, 0:1]), deps=[d_, zeps])
            f_ = P.step("vector", lambda eng, c=c: eng.reciprocal(
                stat["rstd"][:, c:c + 1], stat["rstd"][:, c:c + 1]), deps=[e_])
            g_ = P.step("vector", lambda eng, c=c: eng.scalar_tensor_tensor(
                stat["nmb"][:, c:c + 1], stat["mean"][:, c:c + 1], -1.0,
                stat["rstd"][:, c:c + 1], ALU.mult, ALU.mult), deps=[f_])
            ln_rdy[par][ts] = g_

    def emit_ln_apply(row0, par, q="gpsimd", ptrx=None, ptrx_free=None):
        for ts in range(NCH):
            sl = ts % 2
            c = par * NCH + ts
            lx = P.step(q, lambda eng, r=row0 + ts * CH, sl=sl: eng.dma_start(
                out=x_ln[sl][:], in_=xw[r:r + CH, :]), deps=x_ln_free[sl],
                dma=True, exempt=True, dma_sem=f"d_xl_{q}{sl}")
            x_ln_free[sl] = []
            tm = tm_scr2[ts % 2]
            ap_ = P.step("scalar", lambda eng, sl=sl, c=c, tm=tm: eng.activation(
                tm[:], x_ln[sl][:], AF.Identity,
                bias=stat["nmb"][:, c:c + 1], scale=stat["rstd"][:, c:c + 1]),
                deps=[lx, ln_rdy[par][ts]] + tm_free[ts % 2])
            x_ln_free[sl] = [ap_]
            tm_free[ts % 2] = []
            for hf in range(2):
                tt = P.step("tensor", lambda eng, hf=hf, tm=tm: [eng.transpose(
                    ptrx[hf][:, i * 128:(i + 1) * 128],
                    tm[:, (hf * 8 + i) * 128:(hf * 8 + i + 1) * 128],
                    ident_bf[:]) for i in range(8)],
                    deps=[ap_, cast_loads[0], ptrx_free[hf]])
                tm_free[ts % 2].append(tt)
                dst = xn_fm[:, hf * 8:(hf + 1) * 8, ts * CH:(ts + 1) * CH]
                ev = P.step("vector", lambda eng, hf=hf, dst=dst: eng.tensor_copy(
                    dst, ptrx[hf][:].rearrange("p (a b) -> p a b", a=8)),
                    deps=[tt, xn_w_dep[0]])
                ptrx_free[hf] = ev
            xn_done[ts] = ev

    def emit_qkv(ws, rng=None):
        rng = rng or ring_b
        pacc_free = [None, None]
        for wname in ws:
            dst = {"q": q_fm, "k": k_fm, "v": v_fm}[wname]
            for fo in range(KC):
                s, slot, lt = rng.load(cache[wname][fo], dep=cache_tok[(wname, fo)])
                pa = fo % 2
                mm = P.step("tensor", lambda eng, slot=slot, pa=pa: [eng.matmul(
                    pacc[pa][:], slot[:, kc * 128:(kc + 1) * 128],
                    xn_fm[:, kc, :], start=(kc == 0), stop=(kc == KC - 1))
                    for kc in range(KC)],
                    deps=[lt, pacc_free[pa]] + xn_done)
                rng.consumed(s, mm)
                etok = P.step("scalar", lambda eng, dst=dst, fo=fo, pa=pa:
                              eng.activation(dst[:, fo, :], pacc[pa][:], AF.Silu),
                              deps=[mm] + ([kf_w_dep[0]] if wname == "k" else []))
                pacc_free[pa] = etok
                prod[(wname, fo)] = etok

    def emit_scan(st):
        for ch in range(NCH):
            sl = ch % 2
            lks = P.step("gpsimd", lambda eng, st=st, ch=ch, sl=sl: eng.dma_start(
                out=ksc2[sl][:], in_=sp_ks[st, ch]),
                deps=[ksc_free[sl], sp_done[0]], dma=True, exempt=True,
                dma_sem=f"d_lks{sl}")
            lv = P.step("gpsimd", lambda eng, st=st, ch=ch, sl=sl: eng.dma_start(
                out=v_fm[:, 2 + sl, :], in_=sp_v[st, ch]),
                deps=[vt_free[sl], sp_done[0]], dma=True, exempt=True,
                dma_sem=f"d_lv{sl}")
            for g4 in range(4):
                hs = list(range(g4 * 4, g4 * 4 + 4))
                cols = slice(g4 * 512, (g4 + 1) * 512)
                pi = g4 % 2
                smm = P.step("tensor", lambda eng, ch=ch, hs=hs, pi=pi: [eng.matmul(
                    psc[pi][:, i * 128:(i + 1) * 128],
                    k_fm[:, h, ch * CH:(ch + 1) * CH],
                    q_fm[:, h, ch * CH:(ch + 1) * CH],
                    start=True, stop=True)
                    for i, h in enumerate(hs)],
                    deps=[psc_free[pi], kf_tok[0]] + [prod[("q", h)] for h in hs])
                sev = P.step("vector", lambda eng, g4=g4, cols=cols, pi=pi:
                             eng.tensor_tensor(
                                 scT[g4 % 2][:], psc[pi][:], maskT[:, cols], ALU.mult),
                             deps=[smm, init_vec[-1], scT_free[g4 % 2]])
                psc_free[pi] = sev
                pp = po[g4 % 2]
                omm = P.step("tensor", lambda eng, g4=g4, ch=ch, hs=hs, pp=pp, sl=sl: [
                    eng.matmul(pp[0][:, i * 128:(i + 1) * 128],
                               scT[g4 % 2][:, i * 128:(i + 1) * 128],
                               v_fm[:, 2 + sl, h * 128:(h + 1) * 128],
                               start=True, stop=True)
                    for i, h in enumerate(hs)] + [
                    eng.matmul(pp[1][:, i * 128:(i + 1) * 128],
                               q_fm[:, h, ch * CH:(ch + 1) * CH],
                               S_bf[:, h * 128:(h + 1) * 128],
                               start=True, stop=True)
                    for i, h in enumerate(hs)],
                    deps=[sev, lv, sbf_tok[g4], po_free[g4 % 2]])
                otm_ = P.step("scalar", lambda eng, hs=hs, pp=pp, g4=g4: [
                    eng.activation(o_tmp[g4 % 2][:, i * 128:(i + 1) * 128],
                                   pp[1][:, i * 128:(i + 1) * 128],
                                   AF.Identity, scale=qd_cols[:, h:h + 1])
                    for i, h in enumerate(hs)][-1],
                    deps=[omm, t3, otmp_free[g4 % 2]])
                oev = P.step("vector", lambda eng, ch=ch, cols=cols, pp=pp, g4=g4:
                             eng.tensor_tensor(o_tm[:, ch, cols], pp[0][:],
                                               o_tmp[g4 % 2][:], ALU.add),
                             deps=[omm, otm_])
                po_free[g4 % 2] = oev
                otmp_free[g4 % 2] = oev
                scT_free[g4 % 2] = omm
                umm = P.step("tensor", lambda eng, hs=hs, sl=sl: [eng.matmul(
                    pupd[:, i * 128:(i + 1) * 128],
                    ksc2[sl][:, h * 128:(h + 1) * 128],
                    v_fm[:, 2 + sl, h * 128:(h + 1) * 128],
                    start=True, stop=True)
                    for i, h in enumerate(hs)],
                    deps=[lks, lv, pupd_free[0]])
                ksc_free[sl] = umm
                vt_free[sl] = umm
                uev = P.step("vector", lambda eng, hs=hs: [eng.scalar_tensor_tensor(
                    S_m[:, h * 128:(h + 1) * 128], S_m[:, h * 128:(h + 1) * 128],
                    cd_cols[:, h:h + 1], pupd[:, i * 128:(i + 1) * 128],
                    ALU.mult, ALU.add)
                    for i, h in enumerate(hs)][-1], deps=[umm, s3, zs])
                pupd_free[0] = uev
                sb_ = P.step("scalar", lambda eng, cols=cols: eng.activation(
                    S_bf[:, cols], S_m[:, cols], AF.Identity), deps=[uev])
                sbf_tok[g4] = sb_
            scan_kf_last[0] = smm
            o_done[ch] = oev

    def emit_phase1(kst):
        nonlocal pacc
        row0 = kst * ST
        par = kst % 2
        is_sl = kst >= NLST
        sp = kst - NLST
        P.new_phase()
        with nc.psum_tensor(pnm("p1trA"), [128, 1024], BF16) as pA, \
             nc.psum_tensor(pnm("p1trB"), [128, 1024], BF16) as pB:
            ptr_l = [pA, pB]
            ptrl_free = [None, None]
            emit_ln_apply(row0, par, q="scalar", ptrx=ptr_l, ptrx_free=ptrl_free)
            if is_sl:
                sxn = P.step("scalar", lambda eng, sp=sp: eng.dma_start(
                    out=sp_xn[sp], in_=xn_fm[:].rearrange("p a b -> p (a b)")),
                    deps=xn_done, dma=True, dma_sem="d_spxn")
                sp_last["xn"] = sxn
                xn_w_dep[0] = sxn
            P.flush()
        with ExitStack() as pctx:
            pv = [pctx.enter_context(nc.psum_tensor(pnm(f"pv{i}"), [128, 512], F32))
                  for i in range(8)]
            pv_free = [None, None]
            vev_tok = [None] * NCH
            for half in range(2):
                mm = None
                for k in range(KC):
                    s, slot, lt = ring_b.load(cache["v"][k], dep=cache_tok[("v", k)])
                    mm = P.step("tensor", lambda eng, slot=slot, k=k, half=half: [
                        eng.matmul(pv[ts2 * 4 + fo4][:],
                                   xn_fm[:, k, (half * 2 + ts2) * CH:(half * 2 + ts2 + 1) * CH],
                                   slot[:, fo4 * 512:(fo4 + 1) * 512],
                                   start=(k == 0), stop=(k == KC - 1),
                                   skip_group_check=True)
                        for ts2 in range(2) for fo4 in range(4)],
                        deps=[lt, pv_free[half % 2]] + xn_done)
                    ring_b.consumed(s, mm)
                evs = []
                for ts2 in range(2):
                    ev = P.step("scalar", lambda eng, half=half, ts2=ts2: [
                        eng.activation(v_fm[:, half * 2 + ts2, fo4 * 512:(fo4 + 1) * 512],
                                       pv[ts2 * 4 + fo4][:], AF.Silu)
                        for fo4 in range(4)][-1],
                        deps=[mm, spv_prev[half * 2 + ts2]])
                    vev_tok[half * 2 + ts2] = ev
                    evs.append(ev)
                pv_free[(half + 1) % 2] = evs[-1]
            if is_sl:
                for ch in range(NCH):
                    sv = P.step("scalar", lambda eng, sp=sp, ch=ch: eng.dma_start(
                        out=sp_v[sp, ch], in_=v_fm[:, ch, :]),
                        deps=[vev_tok[ch]], dma=True, dma_sem=f"d_spv{ch}")
                    sp_last["v"] = sv
                    spv_prev[ch] = sv
            P.flush()
        P.new_phase()
        with nc.psum_tensor(pnm("pacc0"), [128, ST], F32) as p0, \
             nc.psum_tensor(pnm("pacc1"), [128, ST], F32) as p1, \
             nc.psum_tensor(pnm("p1kA"), [128, 1024], BF16) as kA, \
             nc.psum_tensor(pnm("p1kB"), [128, 1024], BF16) as kB, \
             nc.psum_tensor(pnm("pupd1"), [128, 512], F32) as pu_:
            pacc = [p0, p1]
            ptrk = [kA, kB]
            ptrk_free = [None, None]
            pupd_free_l = [None]
            emit_qkv(["k"], rng=ring)
            xn_rd[0] = prod[("k", KC - 1)]
            if is_sl:
                skf = P.step("scalar", lambda eng, sp=sp: eng.dma_start(
                    out=sp_kf[sp], in_=k_fm[:].rearrange("p a b -> p (a b)")),
                    deps=[prod[("k", fo)] for fo in range(KC)],
                    dma=True, dma_sem="d_spkf")
                sp_last["kf"] = skf
                kf_w_dep[0] = skf
            for ch in range(NCH):
                sl = ch % 2
                kevs = [None, None]
                for hf in range(2):
                    hs = range(hf * 8, (hf + 1) * 8)
                    tt = P.step("tensor", lambda eng, hf=hf, ch=ch: [eng.transpose(
                        ptrk[hf][:, i * 128:(i + 1) * 128],
                        k_fm[:, hf * 8 + i, ch * CH:(ch + 1) * CH], ident_bf[:])
                        for i in range(8)],
                        deps=[ptrk_free[hf]] + [prod[("k", h)] for h in hs])
                    kev = P.step("scalar", lambda eng, hf=hf, sl=sl: [eng.activation(
                        ksc2[sl][:, (hf * 8 + i) * 128:(hf * 8 + i + 1) * 128],
                        ptrk[hf][:, i * 128:(i + 1) * 128], AF.Identity,
                        scale=kd_cols[:, hf * 8 + i:hf * 8 + i + 1])
                        for i in range(8)][-1],
                        deps=[tt, s2, ksc_free[sl]])
                    kevs[hf] = kev
                    ptrk_free[hf] = kev
                last_cons = kevs[1]
                if not is_sl:
                    for g4 in range(4):
                        hs = list(range(g4 * 4, g4 * 4 + 4))
                        umm = P.step("tensor", lambda eng, hs=hs, sl=sl, ch=ch: [
                            eng.matmul(pu_[:, i * 128:(i + 1) * 128],
                                       ksc2[sl][:, h * 128:(h + 1) * 128],
                                       v_fm[:, ch, h * 128:(h + 1) * 128],
                                       start=True, stop=True)
                            for i, h in enumerate(hs)],
                            deps=[kevs[g4 // 2], vev_tok[ch], pupd_free_l[0]])
                        uev = P.step("vector", lambda eng, hs=hs: [
                            eng.scalar_tensor_tensor(
                                S_m[:, h * 128:(h + 1) * 128],
                                S_m[:, h * 128:(h + 1) * 128],
                                cd_cols[:, h:h + 1], pu_[:, i * 128:(i + 1) * 128],
                                ALU.mult, ALU.add)
                            for i, h in enumerate(hs)][-1], deps=[umm, s3, zs])
                        pupd_free_l[0] = uev
                    last_cons = umm
                ksc_free[sl] = last_cons
                scan_kf_last[0] = kevs[1]
                if is_sl:
                    sks = P.step("scalar", lambda eng, sp=sp, ch=ch, sl=sl: eng.dma_start(
                        out=sp_ks[sp, ch], in_=ksc2[sl][:]),
                        deps=[kevs[1]], dma=True, dma_sem=f"d_spks{sl}")
                    sp_last["ks"] = sks
                    spks_prev[sl] = sks
                    ksc_free[sl] = [last_cons, sks]
            P.flush()

    def epi_gate_pieces(scan_q, epi):
        epi["ogt"] = []
        epi["pgt_free"] = [None]
        epi["sig_free"] = [None, None]

        def p0():
            g1mm = P.step("tensor", lambda eng: [eng.matmul(
                pg1[:], wg1_t[:, kc, :], xn_fm[:, kc, :],
                start=(kc == 0), stop=(kc == KC - 1)) for kc in range(KC)],
                deps=[wg1_tok] + xn_done + scan_q)
            epi["g1ev"] = P.step("scalar", lambda eng: eng.activation(
                g1_fm[:], pg1[:], AF.Identity), deps=[g1mm])
            epi["pgt_free"][0] = g1mm
            g1ev_out[0] = epi["g1ev"]

        pieces = [p0]
        for ts in range(NCH):
            for fo4 in range(4):
                def pf(ts=ts, fo4=fo4):
                    pa = fo4 % 2
                    mm = P.step("tensor", lambda eng, ts=ts, fo4=fo4: eng.matmul(
                        pgt[0][:], g1_fm[:, ts * 128:(ts + 1) * 128],
                        wg2_t[:, fo4 * 512:(fo4 + 1) * 512]),
                        deps=[epi["g1ev"], wg2_tok, epi["pgt_free"][0]] + scan_q)
                    th = P.step("scalar", lambda eng, pa=pa: eng.activation(
                        sig[pa][:], pgt[0][:], AF.Tanh, scale=0.5),
                        deps=[mm, epi["sig_free"][pa]])
                    epi["pgt_free"][0] = th
                    og_dst = og_ap(ts, fo4 * 512, (fo4 + 1) * 512)
                    ot = P.step("vector", lambda eng, ts=ts, fo4=fo4, pa=pa,
                                og_dst=og_dst: eng.scalar_tensor_tensor(
                                    og_dst, sig[pa][:], 1.0,
                                    o_tm[:, ts, fo4 * 512:(fo4 + 1) * 512],
                                    ALU.add, ALU.mult),
                                deps=[th, o_done[ts]])
                    epi["sig_free"][pa] = ot
                    if fo4 == 3:
                        sq_scr = out_hf[:].rearrange("p a b -> p (a b)")
                        sqv = P.step("vector", lambda eng, ts=ts, sq_scr=sq_scr:
                                     eng.tensor_tensor(sq_scr, og_ap(ts, 0, D),
                                                       og_ap(ts, 0, D), ALU.mult),
                                     deps=store_toks[-8:] + [ot])
                        mq = P.step("vector", lambda eng, ts=ts, sq_scr=sq_scr:
                                    eng.reduce_sum(msq[:, ts * H:(ts + 1) * H],
                                                   sq_scr.rearrange(
                                                       "p (h e) -> p h e", h=H),
                                                   AX), deps=[sqv])
                        epi["ogt"].append(mq)
                        epi["mq"] = mq
                pieces.append(pf)
        return pieces

    def emit_epi_norm(epi):
        sq = P.step("scalar", lambda eng: eng.activation(
            rstd_rms[:], msq[:], AF.Sqrt, scale=1.0 / HD,
            bias=eps_col[:, 0:1]), deps=epi["ogt"] + [zeps])
        rc = P.step("vector", lambda eng: eng.reciprocal(
            rstd_rms[:], rstd_rms[:]), deps=[sq])
        for ts in range(NCH):
            tm = tm_scr2[ts % 2]
            on = P.step("vector", lambda eng, ts=ts, tm=tm: [eng.tensor_scalar(
                tm[:, h * 128:(h + 1) * 128], og_ap(ts, h * 128, (h + 1) * 128),
                rstd_rms[:, ts * H + h:ts * H + h + 1], None, ALU.mult)
                for h in range(H)][-1],
                deps=[rc] + tm_free[ts % 2])
            tm_free[ts % 2] = []
            for hf in range(2):
                pb = (ts * 2 + hf) % 2
                tt = P.step("tensor", lambda eng, hf=hf, tm=tm, pb=pb: [eng.transpose(
                    ptrE[pb][:, i * 128:(i + 1) * 128],
                    tm[:, (hf * 8 + i) * 128:(hf * 8 + i + 1) * 128],
                    ident_bf[:]) for i in range(8)],
                    deps=[on, ptrE_free[pb]])
                tm_free[ts % 2].append(tt)
                ev = P.step("vector", lambda eng, ts=ts, hf=hf, pb=pb: [eng.tensor_scalar(
                    onf[:, hf * 8 + i, ts * CH:(ts + 1) * CH],
                    ptrE[pb][:, i * 128:(i + 1) * 128],
                    norm_cols[:, hf * 8 + i:hf * 8 + i + 1], None, ALU.mult)
                    for i in range(8)][-1], deps=[tt, nc_tok])
                ptrE_free[pb] = ev
            onf_done[ts] = ev
        dump_guard[0] = epi["mq"]
        pgt_free_out[0] = epi["pgt_free"][0]
        pgt_free_out[1] = epi["pgt_free"][0]

    def emit_ffn(gc0, gc1, fstate, og_guard=None, pieces=None):
        pge_free = fstate.setdefault("pge_free", [None, None])
        pue_free = fstate.setdefault("pue_free", [None, None])
        hg_free = fstate.setdefault("hg_free", [None, None])
        for gc in range(gc0, gc1):
            htoks = []
            for j in range(4):
                b = gc * 4 + j
                pa = b % 2
                sg_, slotg, ltg = ring.load(cache["gb"][b], dep=cache_tok[("gb", b)])
                mmg = P.step("tensor", lambda eng, slot=slotg, pa=pa: [eng.matmul(
                    pge[pa][:], slot[:, kc * 128:(kc + 1) * 128], xn_fm[:, kc, :],
                    start=(kc == 0), stop=(kc == KC - 1)) for kc in range(KC)],
                    deps=[ltg, pge_free[pa]] + xn_done)
                ring.consumed(sg_, mmg)
                su_, slotu, ltu = ring.load(cache["ub"][b], dep=cache_tok[("ub", b)])
                mmu = P.step("tensor", lambda eng, slot=slotu, pa=pa: [eng.matmul(
                    pue[pa][:], slot[:, kc * 128:(kc + 1) * 128], xn_fm[:, kc, :],
                    start=(kc == 0), stop=(kc == KC - 1)) for kc in range(KC)],
                    deps=[ltu, pue_free[pa]] + xn_done)
                ring.consumed(su_, mmu)
                fstate["last_mm"] = mmu
                hgt = P.step("scalar", lambda eng, pa=pa: eng.activation(
                    h_g[pa][:], pge[pa][:], AF.Silu), deps=[mmg, hg_free[pa]])
                pge_free[pa] = hgt
                ht = P.step("vector", lambda eng, b=b, pa=pa: eng.tensor_tensor(
                    hsp(b), h_g[pa][:], pue[pa][:], ALU.mult),
                    deps=[hgt, mmu, og_guard])
                pue_free[pa] = ht
                hg_free[pa] = ht
                htoks.append(ht)
                if pieces:
                    pieces.pop(0)()
            sq_t = P.step("vector", lambda eng, gc=gc: [eng.tensor_tensor(
                ssq[j][:], hsp(gc * 4 + j), hsp(gc * 4 + j), ALU.mult)
                for j in range(4)][-1], deps=htoks)
            ge_t = P.step("vector", lambda eng: [eng.tensor_tensor(
                ge[p][:], ssq[p[0]][:], ssq[p[1]][:], ALU.is_ge)
                for p in ge][-1], deps=[sq_t])
            wa = P.step("vector", lambda eng: [
                eng.tensor_tensor(ssq[0][:], ge[(0, 1)][:], ge[(0, 2)][:], ALU.add),
                eng.tensor_tensor(ssq[1][:], ge[(1, 2)][:], ge[(1, 3)][:], ALU.add),
                eng.tensor_tensor(ssq[2][:], ge[(2, 3)][:], ge[(0, 2)][:], ALU.subtract),
                eng.tensor_tensor(ssq[3][:], ge[(0, 3)][:], ge[(1, 3)][:], ALU.add),
            ], deps=[ge_t])
            wb = P.step("vector", lambda eng: [
                eng.tensor_tensor(ssq[0][:], ssq[0][:], ge[(0, 3)][:], ALU.add),
                eng.tensor_tensor(ssq[1][:], ssq[1][:], ge[(0, 1)][:], ALU.subtract),
                eng.tensor_tensor(ssq[2][:], ssq[2][:], ge[(1, 2)][:], ALU.subtract),
                eng.tensor_tensor(ssq[3][:], ssq[3][:], ge[(2, 3)][:], ALU.add),
            ], deps=[wa])
            w_t = P.step("vector", lambda eng: [
                eng.tensor_scalar(ssq[0][:], ssq[0][:], 1.5, None, ALU.is_ge),
                eng.tensor_scalar(ssq[1][:], ssq[1][:], 0.5, None, ALU.is_ge),
                eng.tensor_scalar(ssq[2][:], ssq[2][:], -0.5, None, ALU.is_ge),
                eng.tensor_scalar(ssq[3][:], ssq[3][:], 1.5, None, ALU.is_le),
            ], deps=[wb])
            ap_t = P.step("vector", lambda eng, gc=gc: [eng.tensor_tensor(
                hsp(gc * 4 + j), hsp(gc * 4 + j), ssq[j][:], ALU.mult)
                for j in range(4)][-1], deps=[w_t])
            for j in range(4):
                apply_tok[gc * 4 + j] = ap_t

    def emit_final(st, psq):
        nonlocal store_toks
        half_ev = []
        for half in range(2):
            c0 = half * 1024
            last_mm = None
            for dblk in range(KC):
                s, slot, lt = ring.load(cache["o"][dblk], dep=cache_tok[("o", dblk)])
                mm = P.step("tensor", lambda eng, slot=slot, dblk=dblk, c0=c0: [
                    eng.matmul(py[ts2 * 2 + fc][:],
                               onf[:, dblk, ts2 * 128:(ts2 + 1) * 128],
                               slot[:, c0 + fc * 512:c0 + (fc + 1) * 512],
                               start=(dblk == 0), stop=False,
                               skip_group_check=True)
                    for ts2 in range(NCH) for fc in range(2)],
                    deps=[lt] + onf_done + half_ev + psq)
                ring.consumed(s, mm)
                last_mm = mm
                wo_last[0] = mm
            for b in range(NB):
                s, slot, lt = ring.load(cache["db"][b], dep=cache_tok[("db", b)])
                mm = P.step("tensor", lambda eng, slot=slot, b=b, c0=c0: [
                    eng.matmul(py[ts2 * 2 + fc][:],
                               hsp(b)[:, ts2 * 128:(ts2 + 1) * 128],
                               slot[:, c0 + fc * 512:c0 + (fc + 1) * 512],
                               start=False, stop=(b == NB - 1),
                               skip_group_check=True)
                    for ts2 in range(NCH) for fc in range(2)],
                    deps=[lt, apply_tok[b]])
                ring.consumed(s, mm)
                last_mm = mm
            for ts in range(NCH):
                sl = ts % 2
                lx = P.step("gpsimd", lambda eng, ts=ts, c0=c0, sl=sl: eng.dma_start(
                    out=x_h2[sl][:],
                    in_=xw[cfg.LOOK + st * ST + ts * CH: cfg.LOOK + st * ST + (ts + 1) * CH,
                           c0:c0 + 1024]),
                    deps=x_free[sl], dma=True, exempt=True, dma_sem=f"d_x{sl}")
                x_free[sl] = []
                rv = P.step("vector", lambda eng, ts=ts, c0=c0, sl=sl: [
                    eng.tensor_tensor(out_hf[:, ts % 2, fc * 512:(fc + 1) * 512],
                                      py[ts * 2 + fc][:],
                                      x_h2[sl][:, fc * 512:(fc + 1) * 512],
                                      ALU.add) for fc in range(2)][-1],
                    deps=[last_mm, lx, dump_guard[0]] + out_free[ts % 2])
                out_free[ts % 2] = []
                x_free[sl] = [rv]
                half_ev.append(rv)
                so = P.step("scalar", lambda eng, ts=ts, c0=c0, st=st: eng.dma_start(
                    out=yout[st * ST + ts * CH: st * ST + (ts + 1) * CH, c0:c0 + 1024],
                    in_=out_hf[:, ts % 2, :]), deps=[rv], dma=True,
                    dma_sem=f"d_out{ts % 2}")
                out_free[ts % 2] = [so]
                store_toks.append(so)

    _pn = [0]

    def pnm(base):
        _pn[0] += 1
        return f"{base}_{_pn[0]}"

    NKST = NLST + NST
    emit_ln_stats(0, 0, q="scalar")
    for kst in range(NKST):
        emit_phase1(kst)
        if kst + 1 < NKST:
            emit_ln_stats((kst + 1) * ST, (kst + 1) % 2, q="scalar")
    sp_done[0] = ([sp_last["xn"], sp_last["kf"]] + list(spv_prev)
                  + list(spks_prev))
    P.new_phase()
    sbinit = P.step("scalar", lambda eng: eng.activation(
        S_bf[:], S_m[:], AF.Identity), deps=[])
    for g4 in range(4):
        sbf_tok[g4] = sbinit
    P.flush()

    for st in range(NST):
        P.new_phase()
        lxn = P.step("gpsimd", lambda eng, st=st: eng.dma_start(
            out=xn_fm[:].rearrange("p a b -> p (a b)"), in_=sp_xn[st]),
            deps=[xn_rd[0], sp_last["xn"]], dma=True, exempt=True,
            dma_sem="d_lxn")
        for ts in range(NCH):
            xn_done[ts] = lxn
        lkf = P.step("gpsimd", lambda eng, st=st: eng.dma_start(
            out=k_fm[:].rearrange("p a b -> p (a b)"), in_=sp_kf[st]),
            deps=[scan_kf_last[0], sp_last["kf"]], dma=True, exempt=True,
            dma_sem="d_lkf")
        kf_tok[0] = lkf
        with nc.psum_tensor(pnm("pacc0"), [128, ST], F32) as p0, \
             nc.psum_tensor(pnm("pacc1"), [128, ST], F32) as p1:
            pacc = [p0, p1]
            emit_qkv(["q"])
            P.flush()
        P.new_phase()
        with nc.psum_tensor(pnm("psc0"), [128, 512], F32) as ps0, \
             nc.psum_tensor(pnm("psc1"), [128, 512], F32) as ps1, \
             nc.psum_tensor(pnm("poA0"), [128, 512], F32) as poA0, \
             nc.psum_tensor(pnm("poA1"), [128, 512], F32) as poA1, \
             nc.psum_tensor(pnm("poB0"), [128, 512], F32) as poB0, \
             nc.psum_tensor(pnm("poB1"), [128, 512], F32) as poB1, \
             nc.psum_tensor(pnm("pupd"), [128, 512], F32) as pu_:
            psc, pupd = [ps0, ps1], pu_
            po = [[poA0, poA1], [poB0, poB1]]
            psc_free = [None, None]
            pupd_free = [None]
            po_free = [None, None]
            otmp_free = [None, None]
            scT_free = [None, None]
            emit_scan(st)
            P.flush()
        scan_q = P.snapshot()
        P.new_phase()
        with nc.psum_tensor(pnm("pge0"), [128, ST], F32) as g0, \
             nc.psum_tensor(pnm("pge1"), [128, ST], F32) as g1, \
             nc.psum_tensor(pnm("pue0"), [128, ST], F32) as u0, \
             nc.psum_tensor(pnm("pue1"), [128, ST], F32) as u1, \
             nc.psum_tensor(pnm("pg1"), [128, ST], F32) as pg1_, \
             nc.psum_tensor(pnm("pgt0"), [128, 512], F32) as pt0, \
             nc.psum_tensor(pnm("ptrE0"), [128, 1024], BF16) as pt1, \
             nc.psum_tensor(pnm("ptrE1"), [128, 1024], BF16) as pE:
            pge, pue = [g0, g1], [u0, u1]
            pg1, pgt, ptrE = pg1_, [pt0], [pt1, pE]
            ptrE_free = [None, None]
            fstate = {}
            epi = {}
            pieces = epi_gate_pieces(scan_q, epi)
            emit_ffn(0, KC // 2, fstate, pieces=pieces)
            for pc in pieces:
                pc()
            emit_epi_norm(epi)
            og_guard = onf_done[NCH - 1]
            emit_ffn(KC // 2, KC, fstate, og_guard=og_guard)
            xn_rd[0] = fstate["last_mm"]
            P.flush()
            psq = [fstate["pge_free"][0], fstate["pge_free"][1],
                   fstate["pue_free"][0], fstate["pue_free"][1],
                   pgt_free_out[0], g1ev_out[0], ptrE_free[0], ptrE_free[1]]
        with ExitStack() as pctx:
            py = [pctx.enter_context(nc.psum_tensor(pnm(f"py{i}"), [128, 512], F32))
                  for i in range(8)]
            emit_final(st, psq)
            P.flush()
        for _s in range(4):
            ring_b.free[_s].append(wo_last[0])

    P.new_phase()
    P.step("sync", lambda eng: eng.nop(), deps=list(store_toks))
    P.flush()

    ctx.close()
    return nc, dbg


def _prep_inputs(inputs, cfg, core_id):
    x = inputs["L0_attn_x"]
    b = core_id // 4
    s = core_id % 4
    t0 = s * cfg.SLICE
    sl = np.asarray(x[b, t0:t0 + cfg.SLICE], dtype=np.float32)
    if t0 >= cfg.LOOK:
        lb = np.asarray(x[b, t0 - cfg.LOOK:t0], dtype=np.float32)
    elif t0 == 0:
        lb = np.zeros((cfg.LOOK, D), dtype=np.float32)
    else:
        lb = np.concatenate([np.zeros((cfg.LOOK - t0, D), np.float32),
                             np.asarray(x[b, 0:t0], np.float32)], axis=0)
    return np.ascontiguousarray(np.concatenate([lb, sl], axis=0))


def _host_layouts(inputs, cfg):
    r = {}
    wq = np.asarray(inputs["L0_attn_Wq"], np.float32)
    wk = np.asarray(inputs["L0_attn_Wk"], np.float32)
    wv = np.asarray(inputs["L0_attn_Wv"], np.float32)
    wo = np.asarray(inputs["L0_attn_Wo"], np.float32)

    def fo_major(w):
        t = w.reshape(KC, 128, KC, 128)
        return np.ascontiguousarray(t.transpose(2, 1, 0, 3).reshape(KC, 128, D))

    r["wq_b"] = fo_major(wq)
    r["wk_b"] = fo_major(wk)
    r["wv_b"] = np.ascontiguousarray(wv.reshape(KC, 128, D))
    r["wo_b"] = np.ascontiguousarray(wo.reshape(KC, 128, D))
    r["wg1_in"] = np.ascontiguousarray(
        np.asarray(inputs["L0_attn_Wg1"], np.float32).reshape(KC, 128, HD))
    r["wg2_in"] = np.ascontiguousarray(np.asarray(inputs["L0_attn_Wg2"], np.float32))

    def member_major_cols(w):
        t = w.reshape(KC, 128, KC, 128, 4)
        return np.ascontiguousarray(t.transpose(2, 4, 1, 0, 3).reshape(NB, 128, D))

    r["wgb_in"] = member_major_cols(np.asarray(inputs["L0_ffn_w_gate"], np.float32))
    r["wub_in"] = member_major_cols(np.asarray(inputs["L0_ffn_w_up"], np.float32))
    wd = np.asarray(inputs["L0_ffn_w_down"], np.float32)
    r["wdb_in"] = np.ascontiguousarray(
        wd.reshape(KC, 128, 4, D).transpose(0, 2, 1, 3).reshape(NB, 128, D))
    r["normw_in"] = np.ascontiguousarray(
        np.asarray(inputs["L0_attn_norm_w"], np.float32).reshape(KC, 128).T)
    r["ld_in"] = np.ascontiguousarray(np.broadcast_to(
        np.asarray(inputs["L0_attn_log_decay"], np.float32)[None, :], (128, H)))

    idx = np.arange(128, dtype=np.float32)
    diff = idx[None, :] - idx[:, None]
    r["tmask_in"] = np.where(diff >= 0, diff, 1e5).astype(np.float32)
    r["cplus1_in"] = (idx + 1).reshape(128, 1).astype(np.float32)
    r["c127_in"] = (127.0 - idx).reshape(128, 1).astype(np.float32)
    r["ones_in"] = np.ones((128, 128), np.float32)
    r["ident_in"] = np.eye(128, dtype=np.float32)
    return r


def kernel(**inputs):
    cfg = Cfg()
    nc, _ = build_core_program(cfg)
    shared = _host_layouts(inputs, cfg)
    in_maps = []
    for c in range(8):
        m = dict(shared)
        m["xw"] = _prep_inputs(inputs, cfg, c)
        in_maps.append(m)
    res = run_bass_kernel_spmd(nc, in_maps, core_ids=list(range(8)))
    shards = [res.results[c]["yout"] for c in range(8)]
    out = np.empty((B, N, D), dtype=np.float32)
    for c in range(8):
        b, s = c // 4, c % 4
        out[b, s * cfg.SLICE:(s + 1) * cfg.SLICE] = shards[c]
    return out


# version 13
# speedup vs baseline: 1.0694x; 1.0694x over previous
import numpy as np
from contextlib import ExitStack

import concourse.bass as bass
import concourse.mybir as mybir
from concourse.bass_utils import run_bass_kernel_spmd

F32 = mybir.dt.float32
BF16 = mybir.dt.bfloat16
AF = mybir.ActivationFunctionType
ALU = mybir.AluOpType
AX = mybir.AxisListType.X

ENGINES = ("sync", "gpsimd", "tensor", "vector", "scalar")

B, N, D, H, HD = 2, 16384, 2048, 16, 128
I = 8192
KC = D // 128
NB = I // 128
EPS = 1e-5
CH = 128


class Cfg:
    def __init__(self, slice_tokens=4096, look_tokens=4096, st=512):
        self.SLICE = slice_tokens
        self.LOOK = look_tokens
        self.ST = st
        self.NCH = st // CH
        self.NST = slice_tokens // st
        self.NLST = look_tokens // st
        self.WIN = self.LOOK + self.SLICE


class Prog:
    def __init__(self, nc, block, ctx, debug=False):
        self.nc, self.block = nc, block
        self.ctx = ctx
        self.debug = debug
        self.handles = {}
        self.counts = {}
        for en in ENGINES:
            self.sem("c_" + en)
        self.marks = {en: {} for en in ENGINES}
        self.steps = []
        self.barrier_toks = []
        self.phase_first = {en: True for en in ENGINES}

    def sem(self, name):
        if name not in self.handles:
            self.handles[name] = self.ctx.enter_context(self.nc.semaphore(name))
            self.counts[name] = 0
        return name

    def step(self, engine, fn, deps=(), dma=False, n_dma=1, exempt=False,
             dma_sem=None):
        if dma:
            assert dma_sem is not None, "dma steps need a dedicated semaphore"
            sem = self.sem(dma_sem)
        else:
            sem = "c_" + engine
        self.counts[sem] += 16 * n_dma if dma else 1
        flat = []
        for d in deps:
            if d is None:
                continue
            if isinstance(d, list):
                flat.extend(x for x in d if x is not None)
            else:
                flat.append(d)
        if self.phase_first[engine] and not exempt:
            flat = flat + self.barrier_toks
            self.phase_first[engine] = False
        self.steps.append((engine, sem, self.counts[sem], flat, fn, dma, n_dma, exempt))
        return (sem, self.counts[sem])

    def snapshot(self):
        return [("c_" + en, self.counts["c_" + en])
                for en in ("tensor", "vector", "scalar")
                if self.counts["c_" + en] > 0]

    def new_phase(self):
        toks = [("c_" + en, self.counts["c_" + en])
                for en in ("tensor", "vector", "scalar", "gpsimd")]
        self.barrier_toks = [t for t in toks if t[1] > 0]
        if self.debug and "d_dbg" in self.counts:
            self.barrier_toks.append(("d_dbg", self.counts["d_dbg"]))
        self.phase_first = {en: True for en in ENGINES}

    def flush(self):
        steps = self.steps
        self.steps = []
        for en in ENGINES:
            sub = [s for s in steps if s[0] == en]
            if not sub:
                continue

            def body(eng, sub=sub, en=en):
                marks = self.marks[en]
                for (_, sem, val, deps, fn, dma, n_dma, exempt) in sub:
                    for (dsem, dval) in deps:
                        if dval <= 0:
                            continue
                        if marks.get(dsem, 0) >= dval:
                            continue
                        eng.wait_ge(self.handles[dsem], dval)
                        marks[dsem] = dval
                    insts = fn(eng)
                    if insts is None:
                        raise RuntimeError("step fn returned no instruction")
                    if not isinstance(insts, (list, tuple)):
                        insts = [insts]
                    if dma:
                        assert len(insts) == n_dma, (len(insts), n_dma)
                        for inst in insts:
                            inst.then_inc(self.handles[sem], 16)
                    else:
                        insts[-1].then_inc(self.handles[sem], 1)

            getattr(self.block, en)(body)


class Ring:

    def __init__(self, P, slot_aps, name):
        self.P = P
        self.slots = slot_aps
        self.name = name
        self.free = [[] for _ in slot_aps]
        self.i = 0

    def load(self, src_ap, dep=None):
        s = self.i % len(self.slots)
        self.i += 1
        slot = self.slots[s]
        deps = list(self.free[s])
        if dep is not None:
            deps.append(dep)
        tok = self.P.step(
            "sync",
            lambda eng, slot=slot, src=src_ap: eng.dma_start(out=slot, in_=src),
            deps=deps, dma=True, exempt=True, dma_sem=f"d_{self.name}{s}")
        self.free[s] = []
        return s, slot, tok

    def consumed(self, s, tok):
        self.free[s].append(tok)


def build_core_program(cfg, debug_names=()):
    nc = bass.Bass("TRN2", target_bir_lowering=False)
    ST, NCH, NST, NLST = cfg.ST, cfg.NCH, cfg.NST, cfg.NLST

    din = {}
    def dt_in(name, shape):
        din[name] = nc.dram_tensor(name, list(shape), F32, kind="ExternalInput").ap()
        return din[name]

    xw = dt_in("xw", [cfg.WIN, D])
    wq_b = dt_in("wq_b", [KC, 128, D])
    wk_b = dt_in("wk_b", [KC, 128, D])
    wv_b = dt_in("wv_b", [KC, 128, D])
    wo_b = dt_in("wo_b", [KC, 128, D])
    wg1_in = dt_in("wg1_in", [KC, 128, HD])
    wg2_in = dt_in("wg2_in", [128, D])
    wgb_in = dt_in("wgb_in", [NB, 128, D])
    wub_in = dt_in("wub_in", [NB, 128, D])
    wdb_in = dt_in("wdb_in", [NB, 128, D])
    normw_in = dt_in("normw_in", [128, KC])
    ld_in = dt_in("ld_in", [128, H])
    tmask_in = dt_in("tmask_in", [128, 128])
    cplus1_in = dt_in("cplus1_in", [128, 1])
    c127_in = dt_in("c127_in", [128, 1])
    ones_in = dt_in("ones_in", [128, 128])
    ident_in = dt_in("ident_in", [128, 128])

    yout = nc.dram_tensor("yout", [cfg.SLICE, D], F32, kind="ExternalOutput").ap()

    cache = {
        "q": nc.dram_tensor("cache_q", [KC, 128, D], BF16, kind="Internal").ap(),
        "k": nc.dram_tensor("cache_k", [KC, 128, D], BF16, kind="Internal").ap(),
        "v": nc.dram_tensor("cache_v", [KC, 128, D], BF16, kind="Internal").ap(),
        "o": nc.dram_tensor("cache_o", [KC, 128, D], BF16, kind="Internal").ap(),
        "gb": nc.dram_tensor("cache_gb", [NB, 128, D], BF16, kind="Internal").ap(),
        "ub": nc.dram_tensor("cache_ub", [NB, 128, D], BF16, kind="Internal").ap(),
        "db": nc.dram_tensor("cache_db", [NB, 128, D], BF16, kind="Internal").ap(),
    }
    cache_src = {"q": wq_b, "k": wk_b, "v": wv_b, "o": wo_b,
                 "gb": wgb_in, "ub": wub_in, "db": wdb_in}

    sp_xn = nc.dram_tensor("sp_xn", [NST, 128, KC * ST], BF16, kind="Internal").ap()
    sp_kf = nc.dram_tensor("sp_kf", [NST, 128, KC * ST], BF16, kind="Internal").ap()
    sp_ks = nc.dram_tensor("sp_ks", [NST, NCH, 128, D], BF16, kind="Internal").ap()
    sp_v = nc.dram_tensor("sp_v", [NST, NCH, 128, D], BF16, kind="Internal").ap()

    dbg = {}

    ctx = ExitStack()
    sb = lambda name, shape, dt: ctx.enter_context(nc.sbuf_tensor(name, list(shape), dt))

    xn_fm = sb("xn_fm", [128, KC, ST], BF16)
    q_fm = sb("q_fm", [128, KC, ST], BF16)
    k_fm = sb("k_fm", [128, KC, ST], BF16)
    v_fm = sb("v_fm", [128, NCH, D], BF16)
    o_tm = sb("o_tm", [128, NCH, D], BF16)
    onf = sb("onf", [128, KC, ST], BF16)
    S_m = sb("S_m", [128, D], F32)
    S_bf = sb("S_bf", [128, D], BF16)
    x_h2 = [sb(f"x_h2_{i}", [128, 1024], F32) for i in range(2)]
    out_hf = sb("out_hf", [128, 2, 1024], F32)
    ring_slots = [sb(f"ring{i}", [128, D], BF16) for i in range(4)]
    wg1_t = sb("wg1_t", [128, KC, HD], BF16)
    wg2_t = sb("wg2_t", [128, D], BF16)
    norm_cols = sb("norm_cols", [128, KC], F32)
    maskT = sb("maskT", [128, D], BF16)
    ld_rep = sb("ld_rep", [128, H], F32)
    qd_cols = sb("qd_cols", [128, H], F32)
    kd_cols = sb("kd_cols", [128, H], F32)
    cd_cols = sb("cd_cols", [128, H], F32)
    cp1 = sb("cp1", [128, 1], F32)
    c127 = sb("c127", [128, 1], F32)
    tmask_t = sb("tmask_t", [128, 128], F32)
    ident_bf = sb("ident_bf", [128, 128], BF16)
    msk_tmp = sb("msk_tmp", [128, 128], F32)
    stat = {n: sb("st_" + n, [128, 2 * NCH], F32)
            for n in ("sum", "sq", "mean", "ex2", "m2", "var", "rstd", "nmb")}
    tm_scr2 = [sb(f"tm_scr{i}", [128, D], BF16) for i in range(2)]
    ksc2 = [sb(f"ksc2_{i}", [128, D], BF16) for i in range(2)]
    scT = [sb(f"scT{i}", [128, 512], BF16) for i in range(2)]
    o_tmp = [sb(f"o_tmp{i}", [128, 512], BF16) for i in range(2)]
    sig = [sb(f"sig{i}", [128, 512], BF16) for i in range(2)]
    g1_fm = sb("g1_fm", [128, ST], BF16)
    h_g = [sb(f"h_g{i}", [128, 512], BF16) for i in range(2)]
    ssq = [sb(f"ssq{i}", [128, 512], BF16) for i in range(4)]
    ge = {p: sb(f"ge{p[0]}{p[1]}", [128, 512], BF16)
          for p in ((0, 1), (0, 2), (0, 3), (1, 2), (1, 3), (2, 3))}
    msq = sb("msq", [128, NCH * H], F32)
    rstd_rms = sb("rstd_rms", [128, NCH * H], F32)
    eps_col = sb("eps_col", [128, 1], F32)
    x_ln = [sb(f"x_ln{i}", [128, D], F32) for i in range(2)]

    block = ctx.enter_context(nc.Block())
    P = Prog(nc, block, ctx, debug=bool(debug_names))

    cache_tok = {}
    prod = {}
    o_done = [None] * NCH
    onf_done = [None] * NCH
    apply_tok = [None] * NB
    store_toks = []
    out_free = [[] for _ in range(2)]
    x_free = [[] for _ in range(2)]
    sbf_tok = [None] * 4
    xn_done = [None] * NCH
    tm_free = [[], []]
    x_ln_free = [[], []]
    dump_guard = [None]
    pgt_free_out = [None, None]
    g1ev_out = [None]
    wo_last = [None]
    sp_last = {"xn": None, "kf": None, "ks": None, "v": None}
    sp_done = [[]]
    kf_tok = [None]
    scan_kf_last = [None]
    xn_rd = [None]
    spv_prev = [None] * 4
    spks_prev = [None] * 2
    xn_w_dep = [None]
    kf_w_dep = [None]
    ksc_free = [None, None]
    vt_free = [None, None]

    ring = Ring(P, [t[:] for t in ring_slots], "ring")
    onf_flat = onf[:].rearrange("p a b -> p (a b)")
    ring_b = Ring(P, [onf_flat[:, i * 2048:(i + 1) * 2048] for i in range(4)], "ringB")

    def og_ap(ts, c0, c1):
        flat = v_fm[:].rearrange("p a b -> p (a b)")
        return flat[:, ts * D + c0: ts * D + c1]

    def hsp(b):
        if b < 16:
            return q_fm[:, b, :]
        if b < 32:
            return k_fm[:, b - 16, :]
        if b < 48:
            return v_fm[:].rearrange("p a b -> p (a b)").rearrange(
                "p (c e) -> p c e", c=KC)[:, b - 32, :]
        return o_tm[:].rearrange("p a b -> p (a b)").rearrange(
            "p (c e) -> p c e", c=KC)[:, b - 48, :]

    def tap(name, ap, deps):
        if name not in debug_names:
            return
        shape = list(ap.shape)
        out = nc.dram_tensor("dbg_" + name, shape, F32, kind="ExternalOutput").ap()
        dbg[name] = "dbg_" + name
        P.step("scalar", lambda eng, out=out, ap=ap: eng.dma_start(out=out, in_=ap),
               deps=deps, dma=True, dma_sem="d_dbg")

    P.new_phase()
    for dst_t, src in ((ld_rep, ld_in), (cp1, cplus1_in), (c127, c127_in),
                       (tmask_t, tmask_in)):
        small_tok = P.step(
            "sync", lambda eng, d=dst_t, s=src: eng.dma_start(out=d[:], in_=s),
            dma=True, exempt=True, dma_sem="d_ini_s")
    small_loads = [small_tok]
    for dst_t, src in ((ident_bf, ident_in), (wg2_t, wg2_in)):
        cast_tok = P.step(
            "gpsimd", lambda eng, d=dst_t, s=src: eng.dma_start(out=d[:], in_=s),
            dma=True, exempt=True, dma_sem="d_ini_g")
    nc_tok = P.step("sync", lambda eng: eng.dma_start(out=norm_cols[:], in_=normw_in),
                    dma=True, exempt=True, dma_sem="d_ini_n")
    for kc in range(KC):
        cast_tok = P.step(
            "gpsimd", lambda eng, kc=kc: eng.dma_start(out=wg1_t[:, kc, :], in_=wg1_in[kc]),
            dma=True, exempt=True, dma_sem="d_ini_g")
    cast_loads = [cast_tok, cast_tok]
    wg1_tok = cast_tok
    wg2_tok = cast_tok

    t1 = P.step("vector", lambda eng: eng.tensor_scalar(
        qd_cols[:], ld_rep[:], cp1[:, 0:1], None, ALU.mult), deps=small_loads)
    t2 = P.step("vector", lambda eng: eng.tensor_scalar(
        kd_cols[:], ld_rep[:], c127[:, 0:1], None, ALU.mult), deps=small_loads)
    s1 = P.step("scalar", lambda eng: eng.activation(
        qd_cols[:], qd_cols[:], AF.Exp), deps=[t1])
    t3 = P.step("vector", lambda eng: eng.tensor_scalar(
        qd_cols[:], qd_cols[:], 0.5, None, ALU.mult), deps=[s1])
    s2 = P.step("scalar", lambda eng: eng.activation(
        kd_cols[:], kd_cols[:], AF.Exp), deps=[t2])
    s3 = P.step("scalar", lambda eng: eng.activation(
        cd_cols[:], ld_rep[:], AF.Exp, scale=float(CH)), deps=small_loads)
    mask_toks = []
    for h in range(H):
        tv = P.step("vector", lambda eng, h=h: eng.tensor_scalar(
            msk_tmp[:], tmask_t[:], ld_rep[:, h:h + 1], None, ALU.mult),
            deps=small_loads + ([mask_toks[-1]] if mask_toks else []))
        tsc = P.step("scalar", lambda eng, h=h: eng.activation(
            maskT[:, h * 128:(h + 1) * 128], msk_tmp[:], AF.Exp), deps=[tv])
        mask_toks.append(tsc)
    mhalf = P.step("vector", lambda eng: eng.tensor_scalar(
        maskT[:], maskT[:], 0.5, None, ALU.mult), deps=[mask_toks[-1]])
    init_vec = [t3, s2, s3, mhalf]
    zs = P.step("vector", lambda eng: eng.memset(S_m[:], 0.0), deps=[])
    zeps = P.step("vector", lambda eng: eng.memset(eps_col[:], EPS), deps=[])
    zb = P.step("scalar", lambda eng: eng.memzero(S_bf[:]), deps=[])
    for g4 in range(4):
        sbf_tok[g4] = zb

    worder = (["k", "v"] if cfg.NLST else []) + ["q"]
    worder += [w for w in ("k", "v") if w not in worder]
    worder += ["gb", "ub", "db", "o"]
    for wname in worder:
        nblk = NB if wname in ("gb", "ub", "db") else KC
        for b in range(nblk):
            wtok = P.step(
                "gpsimd",
                lambda eng, wn=wname, b=b: eng.dma_start(
                    out=cache[wn][b], in_=cache_src[wn][b]),
                dma=True, exempt=True, dma_sem=f"d_w_{wname}")
        for b in range(nblk):
            cache_tok[(wname, b)] = wtok

    ln_rdy = [[None] * NCH, [None] * NCH]

    def emit_ln_stats(row0, par, q="gpsimd"):
        for ts in range(NCH):
            sl = ts % 2
            c = par * NCH + ts
            lx = P.step(q, lambda eng, r=row0 + ts * CH, sl=sl: eng.dma_start(
                out=x_ln[sl][:], in_=xw[r:r + CH, :]), deps=x_ln_free[sl],
                dma=True, exempt=True, dma_sem=f"d_xl_{q}{sl}")
            x_ln_free[sl] = []
            tsum = P.step("vector", lambda eng, sl=sl, c=c: eng.reduce_sum(
                stat["sum"][:, c:c + 1], x_ln[sl][:], AX), deps=[lx])
            tsq = P.step("scalar", lambda eng, sl=sl, c=c: eng.activation(
                x_ln[sl][:], x_ln[sl][:], AF.Square,
                accum_out=stat["sq"][:, c:c + 1]), deps=[lx, tsum])
            x_ln_free[sl] = [tsq]
            a = P.step("vector", lambda eng, c=c: eng.tensor_scalar(
                stat["mean"][:, c:c + 1], stat["sum"][:, c:c + 1],
                1.0 / D, None, ALU.mult), deps=[tsum])
            b_ = P.step("vector", lambda eng, c=c: eng.tensor_scalar(
                stat["ex2"][:, c:c + 1], stat["sq"][:, c:c + 1],
                1.0 / D, None, ALU.mult), deps=[tsq])
            c_ = P.step("scalar", lambda eng, c=c: eng.activation(
                stat["m2"][:, c:c + 1], stat["mean"][:, c:c + 1],
                AF.Square), deps=[a])
            d_ = P.step("vector", lambda eng, c=c: eng.tensor_tensor(
                stat["var"][:, c:c + 1], stat["ex2"][:, c:c + 1],
                stat["m2"][:, c:c + 1], ALU.subtract), deps=[b_, c_])
            e_ = P.step("scalar", lambda eng, c=c: eng.activation(
                stat["rstd"][:, c:c + 1], stat["var"][:, c:c + 1],
                AF.Sqrt, bias=eps_col[:, 0:1]), deps=[d_, zeps])
            f_ = P.step("vector", lambda eng, c=c: eng.reciprocal(
                stat["rstd"][:, c:c + 1], stat["rstd"][:, c:c + 1]), deps=[e_])
            g_ = P.step("vector", lambda eng, c=c: eng.scalar_tensor_tensor(
                stat["nmb"][:, c:c + 1], stat["mean"][:, c:c + 1], -1.0,
                stat["rstd"][:, c:c + 1], ALU.mult, ALU.mult), deps=[f_])
            ln_rdy[par][ts] = g_

    def emit_ln_apply(row0, par, q="gpsimd", ptrx=None, ptrx_free=None):
        for ts in range(NCH):
            sl = ts % 2
            c = par * NCH + ts
            lx = P.step(q, lambda eng, r=row0 + ts * CH, sl=sl: eng.dma_start(
                out=x_ln[sl][:], in_=xw[r:r + CH, :]), deps=x_ln_free[sl],
                dma=True, exempt=True, dma_sem=f"d_xl_{q}{sl}")
            x_ln_free[sl] = []
            tm = tm_scr2[ts % 2]
            ap_ = P.step("scalar", lambda eng, sl=sl, c=c, tm=tm: eng.activation(
                tm[:], x_ln[sl][:], AF.Identity,
                bias=stat["nmb"][:, c:c + 1], scale=stat["rstd"][:, c:c + 1]),
                deps=[lx, ln_rdy[par][ts]] + tm_free[ts % 2])
            x_ln_free[sl] = [ap_]
            tm_free[ts % 2] = []
            for hf in range(2):
                tt = P.step("tensor", lambda eng, hf=hf, tm=tm: [eng.transpose(
                    ptrx[hf][:, i * 128:(i + 1) * 128],
                    tm[:, (hf * 8 + i) * 128:(hf * 8 + i + 1) * 128],
                    ident_bf[:]) for i in range(8)],
                    deps=[ap_, cast_loads[0], ptrx_free[hf]])
                tm_free[ts % 2].append(tt)
                dst = xn_fm[:, hf * 8:(hf + 1) * 8, ts * CH:(ts + 1) * CH]
                ev = P.step("vector", lambda eng, hf=hf, dst=dst: eng.tensor_copy(
                    dst, ptrx[hf][:].rearrange("p (a b) -> p a b", a=8)),
                    deps=[tt, xn_w_dep[0]])
                ptrx_free[hf] = ev
            xn_done[ts] = ev

    def emit_qkv(ws, rng=None):
        rng = rng or ring_b
        pacc_free = [None, None]
        for wname in ws:
            dst = {"q": q_fm, "k": k_fm, "v": v_fm}[wname]
            for fo in range(KC):
                s, slot, lt = rng.load(cache[wname][fo], dep=cache_tok[(wname, fo)])
                pa = fo % 2
                mm = P.step("tensor", lambda eng, slot=slot, pa=pa: [eng.matmul(
                    pacc[pa][:], slot[:, kc * 128:(kc + 1) * 128],
                    xn_fm[:, kc, :], start=(kc == 0), stop=(kc == KC - 1))
                    for kc in range(KC)],
                    deps=[lt, pacc_free[pa]] + xn_done)
                rng.consumed(s, mm)
                etok = P.step("scalar", lambda eng, dst=dst, fo=fo, pa=pa:
                              eng.activation(dst[:, fo, :], pacc[pa][:], AF.Silu),
                              deps=[mm] + ([kf_w_dep[0]] if wname == "k" else []))
                pacc_free[pa] = etok
                prod[(wname, fo)] = etok

    def emit_scan(st):
        for ch in range(NCH):
            sl = ch % 2
            lks = P.step("gpsimd", lambda eng, st=st, ch=ch, sl=sl: eng.dma_start(
                out=ksc2[sl][:], in_=sp_ks[st, ch]),
                deps=[ksc_free[sl], sp_done[0]], dma=True, exempt=True,
                dma_sem=f"d_lks{sl}")
            lv = P.step("gpsimd", lambda eng, st=st, ch=ch, sl=sl: eng.dma_start(
                out=v_fm[:, 2 + sl, :], in_=sp_v[st, ch]),
                deps=[vt_free[sl], sp_done[0]], dma=True, exempt=True,
                dma_sem=f"d_lv{sl}")
            for g4 in range(4):
                hs = list(range(g4 * 4, g4 * 4 + 4))
                cols = slice(g4 * 512, (g4 + 1) * 512)
                pi = g4 % 2
                smm = P.step("tensor", lambda eng, ch=ch, hs=hs, pi=pi: [eng.matmul(
                    psc[pi][:, i * 128:(i + 1) * 128],
                    k_fm[:, h, ch * CH:(ch + 1) * CH],
                    q_fm[:, h, ch * CH:(ch + 1) * CH],
                    start=True, stop=True)
                    for i, h in enumerate(hs)],
                    deps=[psc_free[pi], kf_tok[0]] + [prod[("q", h)] for h in hs])
                sev = P.step("vector", lambda eng, g4=g4, cols=cols, pi=pi:
                             eng.tensor_tensor(
                                 scT[g4 % 2][:], psc[pi][:], maskT[:, cols], ALU.mult),
                             deps=[smm, init_vec[-1], scT_free[g4 % 2]])
                psc_free[pi] = sev
                pp = po[g4 % 2]
                omm = P.step("tensor", lambda eng, g4=g4, ch=ch, hs=hs, pp=pp, sl=sl: [
                    eng.matmul(pp[0][:, i * 128:(i + 1) * 128],
                               scT[g4 % 2][:, i * 128:(i + 1) * 128],
                               v_fm[:, 2 + sl, h * 128:(h + 1) * 128],
                               start=True, stop=True)
                    for i, h in enumerate(hs)] + [
                    eng.matmul(pp[1][:, i * 128:(i + 1) * 128],
                               q_fm[:, h, ch * CH:(ch + 1) * CH],
                               S_bf[:, h * 128:(h + 1) * 128],
                               start=True, stop=True)
                    for i, h in enumerate(hs)],
                    deps=[sev, lv, sbf_tok[g4], po_free[g4 % 2]])
                otm_ = P.step("scalar", lambda eng, hs=hs, pp=pp, g4=g4: [
                    eng.activation(o_tmp[g4 % 2][:, i * 128:(i + 1) * 128],
                                   pp[1][:, i * 128:(i + 1) * 128],
                                   AF.Identity, scale=qd_cols[:, h:h + 1])
                    for i, h in enumerate(hs)][-1],
                    deps=[omm, t3, otmp_free[g4 % 2]])
                oev = P.step("vector", lambda eng, ch=ch, cols=cols, pp=pp, g4=g4:
                             eng.tensor_tensor(o_tm[:, ch, cols], pp[0][:],
                                               o_tmp[g4 % 2][:], ALU.add),
                             deps=[omm, otm_])
                po_free[g4 % 2] = oev
                otmp_free[g4 % 2] = oev
                scT_free[g4 % 2] = omm
                umm = P.step("tensor", lambda eng, hs=hs, sl=sl: [eng.matmul(
                    pupd[:, i * 128:(i + 1) * 128],
                    ksc2[sl][:, h * 128:(h + 1) * 128],
                    v_fm[:, 2 + sl, h * 128:(h + 1) * 128],
                    start=True, stop=True)
                    for i, h in enumerate(hs)],
                    deps=[lks, lv, pupd_free[0]])
                ksc_free[sl] = umm
                vt_free[sl] = umm
                uev = P.step("vector", lambda eng, hs=hs: [eng.scalar_tensor_tensor(
                    S_m[:, h * 128:(h + 1) * 128], S_m[:, h * 128:(h + 1) * 128],
                    cd_cols[:, h:h + 1], pupd[:, i * 128:(i + 1) * 128],
                    ALU.mult, ALU.add)
                    for i, h in enumerate(hs)][-1], deps=[umm, s3, zs])
                pupd_free[0] = uev
                sb_ = P.step("scalar", lambda eng, cols=cols: eng.activation(
                    S_bf[:, cols], S_m[:, cols], AF.Identity), deps=[uev])
                sbf_tok[g4] = sb_
            scan_kf_last[0] = smm
            o_done[ch] = oev

    def emit_phase1(kst):
        nonlocal pacc
        row0 = kst * ST
        par = kst % 2
        is_sl = kst >= NLST
        sp = kst - NLST
        P.new_phase()
        with nc.psum_tensor(pnm("p1trA"), [128, 1024], BF16) as pA, \
             nc.psum_tensor(pnm("p1trB"), [128, 1024], BF16) as pB:
            ptr_l = [pA, pB]
            ptrl_free = [None, None]
            emit_ln_apply(row0, par, q="sync", ptrx=ptr_l, ptrx_free=ptrl_free)
            if is_sl:
                sxn = P.step("scalar", lambda eng, sp=sp: eng.dma_start(
                    out=sp_xn[sp], in_=xn_fm[:].rearrange("p a b -> p (a b)")),
                    deps=xn_done, dma=True, dma_sem="d_spxn")
                sp_last["xn"] = sxn
                xn_w_dep[0] = sxn
            P.flush()
        with ExitStack() as pctx:
            pv = [pctx.enter_context(nc.psum_tensor(pnm(f"pv{i}"), [128, 512], F32))
                  for i in range(8)]
            pv_free = [None, None]
            vev_tok = [None] * NCH
            for half in range(2):
                mm = None
                for k in range(KC):
                    s, slot, lt = ring_b.load(cache["v"][k], dep=cache_tok[("v", k)])
                    mm = P.step("tensor", lambda eng, slot=slot, k=k, half=half: [
                        eng.matmul(pv[ts2 * 4 + fo4][:],
                                   xn_fm[:, k, (half * 2 + ts2) * CH:(half * 2 + ts2 + 1) * CH],
                                   slot[:, fo4 * 512:(fo4 + 1) * 512],
                                   start=(k == 0), stop=(k == KC - 1),
                                   skip_group_check=True)
                        for ts2 in range(2) for fo4 in range(4)],
                        deps=[lt, pv_free[half % 2]] + xn_done)
                    ring_b.consumed(s, mm)
                evs = []
                for ts2 in range(2):
                    ev = P.step("scalar", lambda eng, half=half, ts2=ts2: [
                        eng.activation(v_fm[:, half * 2 + ts2, fo4 * 512:(fo4 + 1) * 512],
                                       pv[ts2 * 4 + fo4][:], AF.Silu)
                        for fo4 in range(4)][-1],
                        deps=[mm, spv_prev[half * 2 + ts2]])
                    vev_tok[half * 2 + ts2] = ev
                    evs.append(ev)
                pv_free[(half + 1) % 2] = evs[-1]
            if is_sl:
                for ch in range(NCH):
                    sv = P.step("scalar", lambda eng, sp=sp, ch=ch: eng.dma_start(
                        out=sp_v[sp, ch], in_=v_fm[:, ch, :]),
                        deps=[vev_tok[ch]], dma=True, dma_sem=f"d_spv{ch}")
                    sp_last["v"] = sv
                    spv_prev[ch] = sv
            P.flush()
        P.new_phase()
        with nc.psum_tensor(pnm("pacc0"), [128, ST], F32) as p0, \
             nc.psum_tensor(pnm("pacc1"), [128, ST], F32) as p1, \
             nc.psum_tensor(pnm("p1kA"), [128, 1024], BF16) as kA, \
             nc.psum_tensor(pnm("p1kB"), [128, 1024], BF16) as kB, \
             nc.psum_tensor(pnm("pupd1"), [128, 512], F32) as pu_:
            pacc = [p0, p1]
            ptrk = [kA, kB]
            ptrk_free = [None, None]
            pupd_free_l = [None]
            emit_qkv(["k"], rng=ring)
            xn_rd[0] = prod[("k", KC - 1)]
            if is_sl:
                skf = P.step("scalar", lambda eng, sp=sp: eng.dma_start(
                    out=sp_kf[sp], in_=k_fm[:].rearrange("p a b -> p (a b)")),
                    deps=[prod[("k", fo)] for fo in range(KC)],
                    dma=True, dma_sem="d_spkf")
                sp_last["kf"] = skf
                kf_w_dep[0] = skf
            for ch in range(NCH):
                sl = ch % 2
                kevs = [None, None]
                for hf in range(2):
                    hs = range(hf * 8, (hf + 1) * 8)
                    tt = P.step("tensor", lambda eng, hf=hf, ch=ch: [eng.transpose(
                        ptrk[hf][:, i * 128:(i + 1) * 128],
                        k_fm[:, hf * 8 + i, ch * CH:(ch + 1) * CH], ident_bf[:])
                        for i in range(8)],
                        deps=[ptrk_free[hf]] + [prod[("k", h)] for h in hs])
                    kev = P.step("scalar", lambda eng, hf=hf, sl=sl: [eng.activation(
                        ksc2[sl][:, (hf * 8 + i) * 128:(hf * 8 + i + 1) * 128],
                        ptrk[hf][:, i * 128:(i + 1) * 128], AF.Identity,
                        scale=kd_cols[:, hf * 8 + i:hf * 8 + i + 1])
                        for i in range(8)][-1],
                        deps=[tt, s2, ksc_free[sl]])
                    kevs[hf] = kev
                    ptrk_free[hf] = kev
                last_cons = kevs[1]
                if not is_sl:
                    for g4 in range(4):
                        hs = list(range(g4 * 4, g4 * 4 + 4))
                        umm = P.step("tensor", lambda eng, hs=hs, sl=sl, ch=ch: [
                            eng.matmul(pu_[:, i * 128:(i + 1) * 128],
                                       ksc2[sl][:, h * 128:(h + 1) * 128],
                                       v_fm[:, ch, h * 128:(h + 1) * 128],
                                       start=True, stop=True)
                            for i, h in enumerate(hs)],
                            deps=[kevs[g4 // 2], vev_tok[ch], pupd_free_l[0]])
                        uev = P.step("vector", lambda eng, hs=hs: [
                            eng.scalar_tensor_tensor(
                                S_m[:, h * 128:(h + 1) * 128],
                                S_m[:, h * 128:(h + 1) * 128],
                                cd_cols[:, h:h + 1], pu_[:, i * 128:(i + 1) * 128],
                                ALU.mult, ALU.add)
                            for i, h in enumerate(hs)][-1], deps=[umm, s3, zs])
                        pupd_free_l[0] = uev
                    last_cons = umm
                ksc_free[sl] = last_cons
                scan_kf_last[0] = kevs[1]
                if is_sl:
                    sks = P.step("scalar", lambda eng, sp=sp, ch=ch, sl=sl: eng.dma_start(
                        out=sp_ks[sp, ch], in_=ksc2[sl][:]),
                        deps=[kevs[1]], dma=True, dma_sem=f"d_spks{sl}")
                    sp_last["ks"] = sks
                    spks_prev[sl] = sks
                    ksc_free[sl] = [last_cons, sks]
            P.flush()

    def epi_gate_pieces(scan_q, epi):
        epi["ogt"] = []
        epi["pgt_free"] = [None]
        epi["sig_free"] = [None, None]

        def p0():
            g1mm = P.step("tensor", lambda eng: [eng.matmul(
                pg1[:], wg1_t[:, kc, :], xn_fm[:, kc, :],
                start=(kc == 0), stop=(kc == KC - 1)) for kc in range(KC)],
                deps=[wg1_tok] + xn_done + scan_q)
            epi["g1ev"] = P.step("scalar", lambda eng: eng.activation(
                g1_fm[:], pg1[:], AF.Identity), deps=[g1mm])
            epi["pgt_free"][0] = g1mm
            g1ev_out[0] = epi["g1ev"]

        pieces = [p0]
        for ts in range(NCH):
            for fo4 in range(4):
                def pf(ts=ts, fo4=fo4):
                    pa = fo4 % 2
                    mm = P.step("tensor", lambda eng, ts=ts, fo4=fo4: eng.matmul(
                        pgt[0][:], g1_fm[:, ts * 128:(ts + 1) * 128],
                        wg2_t[:, fo4 * 512:(fo4 + 1) * 512]),
                        deps=[epi["g1ev"], wg2_tok, epi["pgt_free"][0]] + scan_q)
                    th = P.step("scalar", lambda eng, pa=pa: eng.activation(
                        sig[pa][:], pgt[0][:], AF.Tanh, scale=0.5),
                        deps=[mm, epi["sig_free"][pa]])
                    epi["pgt_free"][0] = th
                    og_dst = og_ap(ts, fo4 * 512, (fo4 + 1) * 512)
                    ot = P.step("vector", lambda eng, ts=ts, fo4=fo4, pa=pa,
                                og_dst=og_dst: eng.scalar_tensor_tensor(
                                    og_dst, sig[pa][:], 1.0,
                                    o_tm[:, ts, fo4 * 512:(fo4 + 1) * 512],
                                    ALU.add, ALU.mult),
                                deps=[th, o_done[ts]])
                    epi["sig_free"][pa] = ot
                    if fo4 == 3:
                        sq_scr = out_hf[:].rearrange("p a b -> p (a b)")
                        sqv = P.step("vector", lambda eng, ts=ts, sq_scr=sq_scr:
                                     eng.tensor_tensor(sq_scr, og_ap(ts, 0, D),
                                                       og_ap(ts, 0, D), ALU.mult),
                                     deps=store_toks[-8:] + [ot])
                        mq = P.step("vector", lambda eng, ts=ts, sq_scr=sq_scr:
                                    eng.reduce_sum(msq[:, ts * H:(ts + 1) * H],
                                                   sq_scr.rearrange(
                                                       "p (h e) -> p h e", h=H),
                                                   AX), deps=[sqv])
                        epi["ogt"].append(mq)
                        epi["mq"] = mq
                pieces.append(pf)
        return pieces

    def emit_epi_norm(epi):
        sq = P.step("scalar", lambda eng: eng.activation(
            rstd_rms[:], msq[:], AF.Sqrt, scale=1.0 / HD,
            bias=eps_col[:, 0:1]), deps=epi["ogt"] + [zeps])
        rc = P.step("vector", lambda eng: eng.reciprocal(
            rstd_rms[:], rstd_rms[:]), deps=[sq])
        for ts in range(NCH):
            tm = tm_scr2[ts % 2]
            on = P.step("vector", lambda eng, ts=ts, tm=tm: [eng.tensor_scalar(
                tm[:, h * 128:(h + 1) * 128], og_ap(ts, h * 128, (h + 1) * 128),
                rstd_rms[:, ts * H + h:ts * H + h + 1], None, ALU.mult)
                for h in range(H)][-1],
                deps=[rc] + tm_free[ts % 2])
            tm_free[ts % 2] = []
            for hf in range(2):
                pb = (ts * 2 + hf) % 2
                tt = P.step("tensor", lambda eng, hf=hf, tm=tm, pb=pb: [eng.transpose(
                    ptrE[pb][:, i * 128:(i + 1) * 128],
                    tm[:, (hf * 8 + i) * 128:(hf * 8 + i + 1) * 128],
                    ident_bf[:]) for i in range(8)],
                    deps=[on, ptrE_free[pb]])
                tm_free[ts % 2].append(tt)
                ev = P.step("vector", lambda eng, ts=ts, hf=hf, pb=pb: [eng.tensor_scalar(
                    onf[:, hf * 8 + i, ts * CH:(ts + 1) * CH],
                    ptrE[pb][:, i * 128:(i + 1) * 128],
                    norm_cols[:, hf * 8 + i:hf * 8 + i + 1], None, ALU.mult)
                    for i in range(8)][-1], deps=[tt, nc_tok])
                ptrE_free[pb] = ev
            onf_done[ts] = ev
        dump_guard[0] = epi["mq"]
        pgt_free_out[0] = epi["pgt_free"][0]
        pgt_free_out[1] = epi["pgt_free"][0]

    def emit_ffn(gc0, gc1, fstate, og_guard=None, pieces=None):
        pge_free = fstate.setdefault("pge_free", [None, None])
        pue_free = fstate.setdefault("pue_free", [None, None])
        hg_free = fstate.setdefault("hg_free", [None, None])
        for gc in range(gc0, gc1):
            htoks = []
            for j in range(4):
                b = gc * 4 + j
                pa = b % 2
                sg_, slotg, ltg = ring.load(cache["gb"][b], dep=cache_tok[("gb", b)])
                mmg = P.step("tensor", lambda eng, slot=slotg, pa=pa: [eng.matmul(
                    pge[pa][:], slot[:, kc * 128:(kc + 1) * 128], xn_fm[:, kc, :],
                    start=(kc == 0), stop=(kc == KC - 1)) for kc in range(KC)],
                    deps=[ltg, pge_free[pa]] + xn_done)
                ring.consumed(sg_, mmg)
                su_, slotu, ltu = ring.load(cache["ub"][b], dep=cache_tok[("ub", b)])
                mmu = P.step("tensor", lambda eng, slot=slotu, pa=pa: [eng.matmul(
                    pue[pa][:], slot[:, kc * 128:(kc + 1) * 128], xn_fm[:, kc, :],
                    start=(kc == 0), stop=(kc == KC - 1)) for kc in range(KC)],
                    deps=[ltu, pue_free[pa]] + xn_done)
                ring.consumed(su_, mmu)
                fstate["last_mm"] = mmu
                hgt = P.step("scalar", lambda eng, pa=pa: eng.activation(
                    h_g[pa][:], pge[pa][:], AF.Silu), deps=[mmg, hg_free[pa]])
                pge_free[pa] = hgt
                ht = P.step("vector", lambda eng, b=b, pa=pa: eng.tensor_tensor(
                    hsp(b), h_g[pa][:], pue[pa][:], ALU.mult),
                    deps=[hgt, mmu, og_guard])
                pue_free[pa] = ht
                hg_free[pa] = ht
                htoks.append(ht)
                if pieces:
                    pieces.pop(0)()
            sq_t = P.step("vector", lambda eng, gc=gc: [eng.tensor_tensor(
                ssq[j][:], hsp(gc * 4 + j), hsp(gc * 4 + j), ALU.mult)
                for j in range(4)][-1], deps=htoks)
            ge_t = P.step("vector", lambda eng: [eng.tensor_tensor(
                ge[p][:], ssq[p[0]][:], ssq[p[1]][:], ALU.is_ge)
                for p in ge][-1], deps=[sq_t])
            wa = P.step("vector", lambda eng: [
                eng.tensor_tensor(ssq[0][:], ge[(0, 1)][:], ge[(0, 2)][:], ALU.add),
                eng.tensor_tensor(ssq[1][:], ge[(1, 2)][:], ge[(1, 3)][:], ALU.add),
                eng.tensor_tensor(ssq[2][:], ge[(2, 3)][:], ge[(0, 2)][:], ALU.subtract),
                eng.tensor_tensor(ssq[3][:], ge[(0, 3)][:], ge[(1, 3)][:], ALU.add),
            ], deps=[ge_t])
            wb = P.step("vector", lambda eng: [
                eng.tensor_tensor(ssq[0][:], ssq[0][:], ge[(0, 3)][:], ALU.add),
                eng.tensor_tensor(ssq[1][:], ssq[1][:], ge[(0, 1)][:], ALU.subtract),
                eng.tensor_tensor(ssq[2][:], ssq[2][:], ge[(1, 2)][:], ALU.subtract),
                eng.tensor_tensor(ssq[3][:], ssq[3][:], ge[(2, 3)][:], ALU.add),
            ], deps=[wa])
            w_t = P.step("vector", lambda eng: [
                eng.tensor_scalar(ssq[0][:], ssq[0][:], 1.5, None, ALU.is_ge),
                eng.tensor_scalar(ssq[1][:], ssq[1][:], 0.5, None, ALU.is_ge),
                eng.tensor_scalar(ssq[2][:], ssq[2][:], -0.5, None, ALU.is_ge),
                eng.tensor_scalar(ssq[3][:], ssq[3][:], 1.5, None, ALU.is_le),
            ], deps=[wb])
            ap_t = P.step("vector", lambda eng, gc=gc: [eng.tensor_tensor(
                hsp(gc * 4 + j), hsp(gc * 4 + j), ssq[j][:], ALU.mult)
                for j in range(4)][-1], deps=[w_t])
            for j in range(4):
                apply_tok[gc * 4 + j] = ap_t

    def emit_final(st, psq):
        nonlocal store_toks
        half_ev = []
        for half in range(2):
            c0 = half * 1024
            last_mm = None
            for dblk in range(KC):
                s, slot, lt = ring.load(cache["o"][dblk], dep=cache_tok[("o", dblk)])
                mm = P.step("tensor", lambda eng, slot=slot, dblk=dblk, c0=c0: [
                    eng.matmul(py[ts2 * 2 + fc][:],
                               onf[:, dblk, ts2 * 128:(ts2 + 1) * 128],
                               slot[:, c0 + fc * 512:c0 + (fc + 1) * 512],
                               start=(dblk == 0), stop=False,
                               skip_group_check=True)
                    for ts2 in range(NCH) for fc in range(2)],
                    deps=[lt] + onf_done + half_ev + psq)
                ring.consumed(s, mm)
                last_mm = mm
                wo_last[0] = mm
            for b in range(NB):
                s, slot, lt = ring.load(cache["db"][b], dep=cache_tok[("db", b)])
                mm = P.step("tensor", lambda eng, slot=slot, b=b, c0=c0: [
                    eng.matmul(py[ts2 * 2 + fc][:],
                               hsp(b)[:, ts2 * 128:(ts2 + 1) * 128],
                               slot[:, c0 + fc * 512:c0 + (fc + 1) * 512],
                               start=False, stop=(b == NB - 1),
                               skip_group_check=True)
                    for ts2 in range(NCH) for fc in range(2)],
                    deps=[lt, apply_tok[b]])
                ring.consumed(s, mm)
                last_mm = mm
            for ts in range(NCH):
                sl = ts % 2
                lx = P.step("gpsimd", lambda eng, ts=ts, c0=c0, sl=sl: eng.dma_start(
                    out=x_h2[sl][:],
                    in_=xw[cfg.LOOK + st * ST + ts * CH: cfg.LOOK + st * ST + (ts + 1) * CH,
                           c0:c0 + 1024]),
                    deps=x_free[sl], dma=True, exempt=True, dma_sem=f"d_x{sl}")
                x_free[sl] = []
                rv = P.step("vector", lambda eng, ts=ts, c0=c0, sl=sl: [
                    eng.tensor_tensor(out_hf[:, ts % 2, fc * 512:(fc + 1) * 512],
                                      py[ts * 2 + fc][:],
                                      x_h2[sl][:, fc * 512:(fc + 1) * 512],
                                      ALU.add) for fc in range(2)][-1],
                    deps=[last_mm, lx, dump_guard[0]] + out_free[ts % 2])
                out_free[ts % 2] = []
                x_free[sl] = [rv]
                half_ev.append(rv)
                so = P.step("scalar", lambda eng, ts=ts, c0=c0, st=st: eng.dma_start(
                    out=yout[st * ST + ts * CH: st * ST + (ts + 1) * CH, c0:c0 + 1024],
                    in_=out_hf[:, ts % 2, :]), deps=[rv], dma=True,
                    dma_sem=f"d_out{ts % 2}")
                out_free[ts % 2] = [so]
                store_toks.append(so)

    _pn = [0]

    def pnm(base):
        _pn[0] += 1
        return f"{base}_{_pn[0]}"

    NKST = NLST + NST
    emit_ln_stats(0, 0, q="sync")
    for kst in range(NKST):
        emit_phase1(kst)
        if kst + 1 < NKST:
            emit_ln_stats((kst + 1) * ST, (kst + 1) % 2, q="sync")
    sp_done[0] = ([sp_last["xn"], sp_last["kf"]] + list(spv_prev)
                  + list(spks_prev))
    P.new_phase()
    sbinit = P.step("scalar", lambda eng: eng.activation(
        S_bf[:], S_m[:], AF.Identity), deps=[])
    for g4 in range(4):
        sbf_tok[g4] = sbinit
    P.flush()

    for st in range(NST):
        P.new_phase()
        lxn = P.step("gpsimd", lambda eng, st=st: eng.dma_start(
            out=xn_fm[:].rearrange("p a b -> p (a b)"), in_=sp_xn[st]),
            deps=[xn_rd[0], sp_last["xn"]], dma=True, exempt=True,
            dma_sem="d_lxn")
        for ts in range(NCH):
            xn_done[ts] = lxn
        lkf = P.step("gpsimd", lambda eng, st=st: eng.dma_start(
            out=k_fm[:].rearrange("p a b -> p (a b)"), in_=sp_kf[st]),
            deps=[scan_kf_last[0], sp_last["kf"]], dma=True, exempt=True,
            dma_sem="d_lkf")
        kf_tok[0] = lkf
        with nc.psum_tensor(pnm("pacc0"), [128, ST], F32) as p0, \
             nc.psum_tensor(pnm("pacc1"), [128, ST], F32) as p1:
            pacc = [p0, p1]
            emit_qkv(["q"])
            P.flush()
        P.new_phase()
        with nc.psum_tensor(pnm("psc0"), [128, 512], F32) as ps0, \
             nc.psum_tensor(pnm("psc1"), [128, 512], F32) as ps1, \
             nc.psum_tensor(pnm("poA0"), [128, 512], F32) as poA0, \
             nc.psum_tensor(pnm("poA1"), [128, 512], F32) as poA1, \
             nc.psum_tensor(pnm("poB0"), [128, 512], F32) as poB0, \
             nc.psum_tensor(pnm("poB1"), [128, 512], F32) as poB1, \
             nc.psum_tensor(pnm("pupd"), [128, 512], F32) as pu_:
            psc, pupd = [ps0, ps1], pu_
            po = [[poA0, poA1], [poB0, poB1]]
            psc_free = [None, None]
            pupd_free = [None]
            po_free = [None, None]
            otmp_free = [None, None]
            scT_free = [None, None]
            emit_scan(st)
            P.flush()
        scan_q = P.snapshot()
        P.new_phase()
        with nc.psum_tensor(pnm("pge0"), [128, ST], F32) as g0, \
             nc.psum_tensor(pnm("pge1"), [128, ST], F32) as g1, \
             nc.psum_tensor(pnm("pue0"), [128, ST], F32) as u0, \
             nc.psum_tensor(pnm("pue1"), [128, ST], F32) as u1, \
             nc.psum_tensor(pnm("pg1"), [128, ST], F32) as pg1_, \
             nc.psum_tensor(pnm("pgt0"), [128, 512], F32) as pt0, \
             nc.psum_tensor(pnm("ptrE0"), [128, 1024], BF16) as pt1, \
             nc.psum_tensor(pnm("ptrE1"), [128, 1024], BF16) as pE:
            pge, pue = [g0, g1], [u0, u1]
            pg1, pgt, ptrE = pg1_, [pt0], [pt1, pE]
            ptrE_free = [None, None]
            fstate = {}
            epi = {}
            pieces = epi_gate_pieces(scan_q, epi)
            emit_ffn(0, KC // 2, fstate, pieces=pieces)
            for pc in pieces:
                pc()
            emit_epi_norm(epi)
            og_guard = onf_done[NCH - 1]
            emit_ffn(KC // 2, KC, fstate, og_guard=og_guard)
            xn_rd[0] = fstate["last_mm"]
            P.flush()
            psq = [fstate["pge_free"][0], fstate["pge_free"][1],
                   fstate["pue_free"][0], fstate["pue_free"][1],
                   pgt_free_out[0], g1ev_out[0], ptrE_free[0], ptrE_free[1]]
        with ExitStack() as pctx:
            py = [pctx.enter_context(nc.psum_tensor(pnm(f"py{i}"), [128, 512], F32))
                  for i in range(8)]
            emit_final(st, psq)
            P.flush()
        for _s in range(4):
            ring_b.free[_s].append(wo_last[0])

    P.new_phase()
    P.step("sync", lambda eng: eng.nop(), deps=list(store_toks))
    P.flush()

    ctx.close()
    return nc, dbg


def _prep_inputs(inputs, cfg, core_id):
    x = inputs["L0_attn_x"]
    b = core_id // 4
    s = core_id % 4
    t0 = s * cfg.SLICE
    sl = np.asarray(x[b, t0:t0 + cfg.SLICE], dtype=np.float32)
    if t0 >= cfg.LOOK:
        lb = np.asarray(x[b, t0 - cfg.LOOK:t0], dtype=np.float32)
    elif t0 == 0:
        lb = np.zeros((cfg.LOOK, D), dtype=np.float32)
    else:
        lb = np.concatenate([np.zeros((cfg.LOOK - t0, D), np.float32),
                             np.asarray(x[b, 0:t0], np.float32)], axis=0)
    return np.ascontiguousarray(np.concatenate([lb, sl], axis=0))


def _host_layouts(inputs, cfg):
    r = {}
    wq = np.asarray(inputs["L0_attn_Wq"], np.float32)
    wk = np.asarray(inputs["L0_attn_Wk"], np.float32)
    wv = np.asarray(inputs["L0_attn_Wv"], np.float32)
    wo = np.asarray(inputs["L0_attn_Wo"], np.float32)

    def fo_major(w):
        t = w.reshape(KC, 128, KC, 128)
        return np.ascontiguousarray(t.transpose(2, 1, 0, 3).reshape(KC, 128, D))

    r["wq_b"] = fo_major(wq)
    r["wk_b"] = fo_major(wk)
    r["wv_b"] = np.ascontiguousarray(wv.reshape(KC, 128, D))
    r["wo_b"] = np.ascontiguousarray(wo.reshape(KC, 128, D))
    r["wg1_in"] = np.ascontiguousarray(
        np.asarray(inputs["L0_attn_Wg1"], np.float32).reshape(KC, 128, HD))
    r["wg2_in"] = np.ascontiguousarray(np.asarray(inputs["L0_attn_Wg2"], np.float32))

    def member_major_cols(w):
        t = w.reshape(KC, 128, KC, 128, 4)
        return np.ascontiguousarray(t.transpose(2, 4, 1, 0, 3).reshape(NB, 128, D))

    r["wgb_in"] = member_major_cols(np.asarray(inputs["L0_ffn_w_gate"], np.float32))
    r["wub_in"] = member_major_cols(np.asarray(inputs["L0_ffn_w_up"], np.float32))
    wd = np.asarray(inputs["L0_ffn_w_down"], np.float32)
    r["wdb_in"] = np.ascontiguousarray(
        wd.reshape(KC, 128, 4, D).transpose(0, 2, 1, 3).reshape(NB, 128, D))
    r["normw_in"] = np.ascontiguousarray(
        np.asarray(inputs["L0_attn_norm_w"], np.float32).reshape(KC, 128).T)
    r["ld_in"] = np.ascontiguousarray(np.broadcast_to(
        np.asarray(inputs["L0_attn_log_decay"], np.float32)[None, :], (128, H)))

    idx = np.arange(128, dtype=np.float32)
    diff = idx[None, :] - idx[:, None]
    r["tmask_in"] = np.where(diff >= 0, diff, 1e5).astype(np.float32)
    r["cplus1_in"] = (idx + 1).reshape(128, 1).astype(np.float32)
    r["c127_in"] = (127.0 - idx).reshape(128, 1).astype(np.float32)
    r["ones_in"] = np.ones((128, 128), np.float32)
    r["ident_in"] = np.eye(128, dtype=np.float32)
    return r


def kernel(**inputs):
    cfg = Cfg()
    nc, _ = build_core_program(cfg)
    shared = _host_layouts(inputs, cfg)
    in_maps = []
    for c in range(8):
        m = dict(shared)
        m["xw"] = _prep_inputs(inputs, cfg, c)
        in_maps.append(m)
    res = run_bass_kernel_spmd(nc, in_maps, core_ids=list(range(8)))
    shards = [res.results[c]["yout"] for c in range(8)]
    out = np.empty((B, N, D), dtype=np.float32)
    for c in range(8):
        b, s = c // 4, c % 4
        out[b, s * cfg.SLICE:(s + 1) * cfg.SLICE] = shards[c]
    return out


# version 14 (final)
# speedup vs baseline: 1.0859x; 1.0154x over previous
import numpy as np
from contextlib import ExitStack

import concourse.bass as bass
import concourse.mybir as mybir
from concourse.bass_utils import run_bass_kernel_spmd

F32 = mybir.dt.float32
BF16 = mybir.dt.bfloat16
AF = mybir.ActivationFunctionType
ALU = mybir.AluOpType
AX = mybir.AxisListType.X

ENGINES = ("sync", "gpsimd", "tensor", "vector", "scalar")

B, N, D, H, HD = 2, 16384, 2048, 16, 128
I = 8192
KC = D // 128
NB = I // 128
EPS = 1e-5
CH = 128


class Cfg:
    def __init__(self, slice_tokens=4096, look_tokens=4096, st=512):
        self.SLICE = slice_tokens
        self.LOOK = look_tokens
        self.ST = st
        self.NCH = st // CH
        self.NST = slice_tokens // st
        self.NLST = look_tokens // st
        self.WIN = self.LOOK + self.SLICE


class Prog:
    def __init__(self, nc, block, ctx, debug=False):
        self.nc, self.block = nc, block
        self.ctx = ctx
        self.debug = debug
        self.handles = {}
        self.counts = {}
        for en in ENGINES:
            self.sem("c_" + en)
        self.marks = {en: {} for en in ENGINES}
        self.steps = []
        self.barrier_toks = []
        self.phase_first = {en: True for en in ENGINES}

    def sem(self, name):
        if name not in self.handles:
            self.handles[name] = self.ctx.enter_context(self.nc.semaphore(name))
            self.counts[name] = 0
        return name

    def step(self, engine, fn, deps=(), dma=False, n_dma=1, exempt=False,
             dma_sem=None):
        if dma:
            assert dma_sem is not None, "dma steps need a dedicated semaphore"
            sem = self.sem(dma_sem)
        else:
            sem = "c_" + engine
        self.counts[sem] += 16 * n_dma if dma else 1
        flat = []
        for d in deps:
            if d is None:
                continue
            if isinstance(d, list):
                flat.extend(x for x in d if x is not None)
            else:
                flat.append(d)
        if self.phase_first[engine] and not exempt:
            flat = flat + self.barrier_toks
            self.phase_first[engine] = False
        self.steps.append((engine, sem, self.counts[sem], flat, fn, dma, n_dma, exempt))
        return (sem, self.counts[sem])

    def snapshot(self):
        return [("c_" + en, self.counts["c_" + en])
                for en in ("tensor", "vector", "scalar")
                if self.counts["c_" + en] > 0]

    def new_phase(self):
        toks = [("c_" + en, self.counts["c_" + en])
                for en in ("tensor", "vector", "scalar", "gpsimd")]
        self.barrier_toks = [t for t in toks if t[1] > 0]
        if self.debug and "d_dbg" in self.counts:
            self.barrier_toks.append(("d_dbg", self.counts["d_dbg"]))
        self.phase_first = {en: True for en in ENGINES}

    def flush(self):
        steps = self.steps
        self.steps = []
        for en in ENGINES:
            sub = [s for s in steps if s[0] == en]
            if not sub:
                continue

            def body(eng, sub=sub, en=en):
                marks = self.marks[en]
                for (_, sem, val, deps, fn, dma, n_dma, exempt) in sub:
                    for (dsem, dval) in deps:
                        if dval <= 0:
                            continue
                        if marks.get(dsem, 0) >= dval:
                            continue
                        eng.wait_ge(self.handles[dsem], dval)
                        marks[dsem] = dval
                    insts = fn(eng)
                    if insts is None:
                        raise RuntimeError("step fn returned no instruction")
                    if not isinstance(insts, (list, tuple)):
                        insts = [insts]
                    if dma:
                        assert len(insts) == n_dma, (len(insts), n_dma)
                        for inst in insts:
                            inst.then_inc(self.handles[sem], 16)
                    else:
                        insts[-1].then_inc(self.handles[sem], 1)

            getattr(self.block, en)(body)


class Ring:

    def __init__(self, P, slot_aps, name):
        self.P = P
        self.slots = slot_aps
        self.name = name
        self.free = [[] for _ in slot_aps]
        self.i = 0

    def load(self, src_ap, dep=None):
        s = self.i % len(self.slots)
        self.i += 1
        slot = self.slots[s]
        deps = list(self.free[s])
        if dep is not None:
            deps.append(dep)
        tok = self.P.step(
            "sync",
            lambda eng, slot=slot, src=src_ap: eng.dma_start(out=slot, in_=src),
            deps=deps, dma=True, exempt=True, dma_sem=f"d_{self.name}{s}")
        self.free[s] = []
        return s, slot, tok

    def consumed(self, s, tok):
        self.free[s].append(tok)


def build_core_program(cfg, debug_names=()):
    nc = bass.Bass("TRN2", target_bir_lowering=False)
    ST, NCH, NST, NLST = cfg.ST, cfg.NCH, cfg.NST, cfg.NLST

    din = {}
    def dt_in(name, shape):
        din[name] = nc.dram_tensor(name, list(shape), F32, kind="ExternalInput").ap()
        return din[name]

    xw = dt_in("xw", [cfg.WIN, D])
    wq_b = dt_in("wq_b", [KC, 128, D])
    wk_b = dt_in("wk_b", [KC, 128, D])
    wv_b = dt_in("wv_b", [KC, 128, D])
    wo_b = dt_in("wo_b", [KC, 128, D])
    wg1_in = dt_in("wg1_in", [KC, 128, HD])
    wg2_in = dt_in("wg2_in", [128, D])
    wgb_in = dt_in("wgb_in", [NB, 128, D])
    wub_in = dt_in("wub_in", [NB, 128, D])
    wdb_in = dt_in("wdb_in", [NB, 128, D])
    normw_in = dt_in("normw_in", [128, KC])
    ld_in = dt_in("ld_in", [128, H])
    tmask_in = dt_in("tmask_in", [128, 128])
    cplus1_in = dt_in("cplus1_in", [128, 1])
    c127_in = dt_in("c127_in", [128, 1])
    ones_in = dt_in("ones_in", [128, 128])
    ident_in = dt_in("ident_in", [128, 128])

    yout = nc.dram_tensor("yout", [cfg.SLICE, D], F32, kind="ExternalOutput").ap()

    cache = {
        "q": nc.dram_tensor("cache_q", [KC, 128, D], BF16, kind="Internal").ap(),
        "k": nc.dram_tensor("cache_k", [KC, 128, D], BF16, kind="Internal").ap(),
        "v": nc.dram_tensor("cache_v", [KC, 128, D], BF16, kind="Internal").ap(),
        "o": nc.dram_tensor("cache_o", [KC, 128, D], BF16, kind="Internal").ap(),
        "gb": nc.dram_tensor("cache_gb", [NB, 128, D], BF16, kind="Internal").ap(),
        "ub": nc.dram_tensor("cache_ub", [NB, 128, D], BF16, kind="Internal").ap(),
        "db": nc.dram_tensor("cache_db", [NB, 128, D], BF16, kind="Internal").ap(),
    }
    cache_src = {"q": wq_b, "k": wk_b, "v": wv_b, "o": wo_b,
                 "gb": wgb_in, "ub": wub_in, "db": wdb_in}

    sp_xn = nc.dram_tensor("sp_xn", [NST, 128, KC * ST], BF16, kind="Internal").ap()
    sp_kf = nc.dram_tensor("sp_kf", [NST, 128, KC * ST], BF16, kind="Internal").ap()
    sp_ks = nc.dram_tensor("sp_ks", [NST, NCH, 128, D], BF16, kind="Internal").ap()
    sp_v = nc.dram_tensor("sp_v", [NST, NCH, 128, D], BF16, kind="Internal").ap()

    dbg = {}

    ctx = ExitStack()
    sb = lambda name, shape, dt: ctx.enter_context(nc.sbuf_tensor(name, list(shape), dt))

    xn_fm = sb("xn_fm", [128, KC, ST], BF16)
    q_fm = sb("q_fm", [128, KC, ST], BF16)
    k_fm = sb("k_fm", [128, KC, ST], BF16)
    v_fm = sb("v_fm", [128, NCH, D], BF16)
    o_tm = sb("o_tm", [128, NCH, D], BF16)
    onf = sb("onf", [128, KC, ST], BF16)
    S_m = sb("S_m", [128, D], F32)
    S_bf = sb("S_bf", [128, D], BF16)
    x_h2 = [sb(f"x_h2_{i}", [128, 1024], F32) for i in range(2)]
    out_hf = sb("out_hf", [128, 2, 1024], F32)
    ring_slots = [sb(f"ring{i}", [128, D], BF16) for i in range(4)]
    wg1_t = sb("wg1_t", [128, KC, HD], BF16)
    wg2_t = sb("wg2_t", [128, D], BF16)
    norm_cols = sb("norm_cols", [128, KC], F32)
    maskT = sb("maskT", [128, D], BF16)
    ld_rep = sb("ld_rep", [128, H], F32)
    qd_cols = sb("qd_cols", [128, H], F32)
    kd_cols = sb("kd_cols", [128, H], F32)
    cd_cols = sb("cd_cols", [128, H], F32)
    cp1 = sb("cp1", [128, 1], F32)
    c127 = sb("c127", [128, 1], F32)
    tmask_t = sb("tmask_t", [128, 128], F32)
    ident_bf = sb("ident_bf", [128, 128], BF16)
    msk_tmp = sb("msk_tmp", [128, 128], F32)
    stat = {n: sb("st_" + n, [128, 2 * NCH], F32)
            for n in ("sum", "sq", "mean", "ex2", "m2", "var", "rstd", "nmb")}
    tm_scr2 = [sb(f"tm_scr{i}", [128, D], BF16) for i in range(2)]
    ksc2 = [sb(f"ksc2_{i}", [128, D], BF16) for i in range(2)]
    scT = [sb(f"scT{i}", [128, 512], BF16) for i in range(2)]
    o_tmp = [sb(f"o_tmp{i}", [128, 512], BF16) for i in range(2)]
    sig = [sb(f"sig{i}", [128, 512], BF16) for i in range(2)]
    g1_fm = sb("g1_fm", [128, ST], BF16)
    h_g = [sb(f"h_g{i}", [128, 512], BF16) for i in range(2)]
    ssq = [sb(f"ssq{i}", [128, 512], BF16) for i in range(4)]
    ge = {p: sb(f"ge{p[0]}{p[1]}", [128, 512], BF16)
          for p in ((0, 1), (0, 2), (0, 3), (1, 2), (1, 3), (2, 3))}
    msq = sb("msq", [128, NCH * H], F32)
    rstd_rms = sb("rstd_rms", [128, NCH * H], F32)
    eps_col = sb("eps_col", [128, 1], F32)
    x_ln = [sb(f"x_ln{i}", [128, D], F32) for i in range(2)]

    block = ctx.enter_context(nc.Block())
    P = Prog(nc, block, ctx, debug=bool(debug_names))

    cache_tok = {}
    prod = {}
    o_done = [None] * NCH
    onf_done = [None] * NCH
    apply_tok = [None] * NB
    store_toks = []
    out_free = [[] for _ in range(2)]
    x_free = [[] for _ in range(2)]
    sbf_tok = [None] * 4
    xn_done = [None] * NCH
    tm_free = [[], []]
    x_ln_free = [[], []]
    dump_guard = [None]
    pgt_free_out = [None, None]
    g1ev_out = [None]
    wo_last = [None]
    sp_last = {"xn": None, "kf": None, "ks": None, "v": None}
    sp_done = [[]]
    kf_tok = [None]
    scan_kf_last = [None]
    xn_rd = [None]
    spv_prev = [None] * 4
    spks_prev = [None] * 2
    sb_guard = [None] * 4
    xn_w_dep = [None]
    kf_w_dep = [None]
    ksc_free = [None, None]
    vt_free = [None, None]

    ring = Ring(P, [t[:] for t in ring_slots], "ring")
    onf_flat = onf[:].rearrange("p a b -> p (a b)")
    ring_b = Ring(P, [onf_flat[:, i * 2048:(i + 1) * 2048] for i in range(4)], "ringB")

    def og_ap(ts, c0, c1):
        flat = v_fm[:].rearrange("p a b -> p (a b)")
        return flat[:, ts * D + c0: ts * D + c1]

    def hsp(b):
        if b < 16:
            return q_fm[:, b, :]
        if b < 32:
            return k_fm[:, b - 16, :]
        if b < 48:
            return v_fm[:].rearrange("p a b -> p (a b)").rearrange(
                "p (c e) -> p c e", c=KC)[:, b - 32, :]
        return o_tm[:].rearrange("p a b -> p (a b)").rearrange(
            "p (c e) -> p c e", c=KC)[:, b - 48, :]

    def tap(name, ap, deps):
        if name not in debug_names:
            return
        shape = list(ap.shape)
        out = nc.dram_tensor("dbg_" + name, shape, F32, kind="ExternalOutput").ap()
        dbg[name] = "dbg_" + name
        P.step("scalar", lambda eng, out=out, ap=ap: eng.dma_start(out=out, in_=ap),
               deps=deps, dma=True, dma_sem="d_dbg")

    P.new_phase()
    for dst_t, src in ((ld_rep, ld_in), (cp1, cplus1_in), (c127, c127_in),
                       (tmask_t, tmask_in)):
        small_tok = P.step(
            "sync", lambda eng, d=dst_t, s=src: eng.dma_start(out=d[:], in_=s),
            dma=True, exempt=True, dma_sem="d_ini_s")
    small_loads = [small_tok]
    for dst_t, src in ((ident_bf, ident_in), (wg2_t, wg2_in)):
        cast_tok = P.step(
            "gpsimd", lambda eng, d=dst_t, s=src: eng.dma_start(out=d[:], in_=s),
            dma=True, exempt=True, dma_sem="d_ini_g")
    nc_tok = P.step("sync", lambda eng: eng.dma_start(out=norm_cols[:], in_=normw_in),
                    dma=True, exempt=True, dma_sem="d_ini_n")
    for kc in range(KC):
        cast_tok = P.step(
            "gpsimd", lambda eng, kc=kc: eng.dma_start(out=wg1_t[:, kc, :], in_=wg1_in[kc]),
            dma=True, exempt=True, dma_sem="d_ini_g")
    cast_loads = [cast_tok, cast_tok]
    wg1_tok = cast_tok
    wg2_tok = cast_tok

    t1 = P.step("vector", lambda eng: eng.tensor_scalar(
        qd_cols[:], ld_rep[:], cp1[:, 0:1], None, ALU.mult), deps=small_loads)
    t2 = P.step("vector", lambda eng: eng.tensor_scalar(
        kd_cols[:], ld_rep[:], c127[:, 0:1], None, ALU.mult), deps=small_loads)
    s1 = P.step("scalar", lambda eng: eng.activation(
        qd_cols[:], qd_cols[:], AF.Exp), deps=[t1])
    t3 = P.step("vector", lambda eng: eng.tensor_scalar(
        qd_cols[:], qd_cols[:], 0.5, None, ALU.mult), deps=[s1])
    s2 = P.step("scalar", lambda eng: eng.activation(
        kd_cols[:], kd_cols[:], AF.Exp), deps=[t2])
    s3 = P.step("scalar", lambda eng: eng.activation(
        cd_cols[:], ld_rep[:], AF.Exp, scale=float(CH)), deps=small_loads)
    mask_toks = []
    for h in range(H):
        tv = P.step("vector", lambda eng, h=h: eng.tensor_scalar(
            msk_tmp[:], tmask_t[:], ld_rep[:, h:h + 1], None, ALU.mult),
            deps=small_loads + ([mask_toks[-1]] if mask_toks else []))
        tsc = P.step("scalar", lambda eng, h=h: eng.activation(
            maskT[:, h * 128:(h + 1) * 128], msk_tmp[:], AF.Exp), deps=[tv])
        mask_toks.append(tsc)
    mhalf = P.step("vector", lambda eng: eng.tensor_scalar(
        maskT[:], maskT[:], 0.5, None, ALU.mult), deps=[mask_toks[-1]])
    init_vec = [t3, s2, s3, mhalf]
    zs = P.step("vector", lambda eng: eng.memset(S_m[:], 0.0), deps=[])
    zeps = P.step("vector", lambda eng: eng.memset(eps_col[:], EPS), deps=[])
    zb = P.step("scalar", lambda eng: eng.memzero(S_bf[:]), deps=[])
    for g4 in range(4):
        sbf_tok[g4] = zb

    worder = (["k", "v"] if cfg.NLST else []) + ["q"]
    worder += [w for w in ("k", "v") if w not in worder]
    worder += ["gb", "ub", "db", "o"]
    for wname in worder:
        nblk = NB if wname in ("gb", "ub", "db") else KC
        for b in range(nblk):
            wtok = P.step(
                "gpsimd",
                lambda eng, wn=wname, b=b: eng.dma_start(
                    out=cache[wn][b], in_=cache_src[wn][b]),
                dma=True, exempt=True, dma_sem=f"d_w_{wname}")
        for b in range(nblk):
            cache_tok[(wname, b)] = wtok

    ln_rdy = [[None] * NCH, [None] * NCH]

    def emit_ln_stats(row0, par, q="gpsimd"):
        for ts in range(NCH):
            sl = ts % 2
            c = par * NCH + ts
            lx = P.step(q, lambda eng, r=row0 + ts * CH, sl=sl: eng.dma_start(
                out=x_ln[sl][:], in_=xw[r:r + CH, :]), deps=x_ln_free[sl],
                dma=True, exempt=True, dma_sem=f"d_xl_{q}{sl}")
            x_ln_free[sl] = []
            tsum = P.step("vector", lambda eng, sl=sl, c=c: eng.reduce_sum(
                stat["sum"][:, c:c + 1], x_ln[sl][:], AX), deps=[lx])
            tsq = P.step("scalar", lambda eng, sl=sl, c=c: eng.activation(
                x_ln[sl][:], x_ln[sl][:], AF.Square,
                accum_out=stat["sq"][:, c:c + 1]), deps=[lx, tsum])
            x_ln_free[sl] = [tsq]
            a = P.step("vector", lambda eng, c=c: eng.tensor_scalar(
                stat["mean"][:, c:c + 1], stat["sum"][:, c:c + 1],
                1.0 / D, None, ALU.mult), deps=[tsum])
            b_ = P.step("vector", lambda eng, c=c: eng.tensor_scalar(
                stat["ex2"][:, c:c + 1], stat["sq"][:, c:c + 1],
                1.0 / D, None, ALU.mult), deps=[tsq])
            c_ = P.step("scalar", lambda eng, c=c: eng.activation(
                stat["m2"][:, c:c + 1], stat["mean"][:, c:c + 1],
                AF.Square), deps=[a])
            d_ = P.step("vector", lambda eng, c=c: eng.tensor_tensor(
                stat["var"][:, c:c + 1], stat["ex2"][:, c:c + 1],
                stat["m2"][:, c:c + 1], ALU.subtract), deps=[b_, c_])
            e_ = P.step("scalar", lambda eng, c=c: eng.activation(
                stat["rstd"][:, c:c + 1], stat["var"][:, c:c + 1],
                AF.Sqrt, bias=eps_col[:, 0:1]), deps=[d_, zeps])
            f_ = P.step("vector", lambda eng, c=c: eng.reciprocal(
                stat["rstd"][:, c:c + 1], stat["rstd"][:, c:c + 1]), deps=[e_])
            g_ = P.step("vector", lambda eng, c=c: eng.scalar_tensor_tensor(
                stat["nmb"][:, c:c + 1], stat["mean"][:, c:c + 1], -1.0,
                stat["rstd"][:, c:c + 1], ALU.mult, ALU.mult), deps=[f_])
            ln_rdy[par][ts] = g_

    def emit_ln_apply(row0, par, q="gpsimd", ptrx=None, ptrx_free=None):
        for ts in range(NCH):
            sl = ts % 2
            c = par * NCH + ts
            lx = P.step(q, lambda eng, r=row0 + ts * CH, sl=sl: eng.dma_start(
                out=x_ln[sl][:], in_=xw[r:r + CH, :]), deps=x_ln_free[sl],
                dma=True, exempt=True, dma_sem=f"d_xl_{q}{sl}")
            x_ln_free[sl] = []
            tm = tm_scr2[ts % 2]
            ap_ = P.step("vector", lambda eng, sl=sl, c=c, tm=tm: eng.tensor_scalar(
                tm[:], x_ln[sl][:], stat["rstd"][:, c:c + 1],
                stat["nmb"][:, c:c + 1], ALU.mult, ALU.add),
                deps=[lx, ln_rdy[par][ts]] + tm_free[ts % 2])
            x_ln_free[sl] = [ap_]
            tm_free[ts % 2] = []
            for hf in range(2):
                tt = P.step("tensor", lambda eng, hf=hf, tm=tm: [eng.transpose(
                    ptrx[hf][:, i * 128:(i + 1) * 128],
                    tm[:, (hf * 8 + i) * 128:(hf * 8 + i + 1) * 128],
                    ident_bf[:]) for i in range(8)],
                    deps=[ap_, cast_loads[0], ptrx_free[hf]])
                tm_free[ts % 2].append(tt)
                dst = xn_fm[:, hf * 8:(hf + 1) * 8, ts * CH:(ts + 1) * CH]
                ev = P.step("vector", lambda eng, hf=hf, dst=dst: eng.tensor_copy(
                    dst, ptrx[hf][:].rearrange("p (a b) -> p a b", a=8)),
                    deps=[tt, xn_w_dep[0]])
                ptrx_free[hf] = ev
            xn_done[ts] = ev

    def emit_qkv(ws, rng=None):
        rng = rng or ring_b
        pacc_free = [None] * len(pacc)
        for wname in ws:
            dst = {"q": q_fm, "k": k_fm, "v": v_fm}[wname]
            for fo in range(KC):
                s, slot, lt = rng.load(cache[wname][fo], dep=cache_tok[(wname, fo)])
                pa = fo % len(pacc)
                mm = P.step("tensor", lambda eng, slot=slot, pa=pa: [eng.matmul(
                    pacc[pa][:], slot[:, kc * 128:(kc + 1) * 128],
                    xn_fm[:, kc, :], start=(kc == 0), stop=(kc == KC - 1))
                    for kc in range(KC)],
                    deps=[lt, pacc_free[pa]] + xn_done)
                rng.consumed(s, mm)
                etok = P.step("scalar", lambda eng, dst=dst, fo=fo, pa=pa:
                              eng.activation(dst[:, fo, :], pacc[pa][:], AF.Silu),
                              deps=[mm] + ([kf_w_dep[0]] if wname == "k" else []))
                pacc_free[pa] = etok
                prod[(wname, fo)] = etok

    def emit_scan(st):
        for ch in range(NCH):
            sl = ch % 2
            lks = P.step("gpsimd", lambda eng, st=st, ch=ch, sl=sl: eng.dma_start(
                out=ksc2[sl][:], in_=sp_ks[st, ch]),
                deps=[ksc_free[sl], sp_done[0]], dma=True, exempt=True,
                dma_sem=f"d_lks{sl}")
            lv = P.step("gpsimd", lambda eng, st=st, ch=ch, sl=sl: eng.dma_start(
                out=v_fm[:, 2 + sl, :], in_=sp_v[st, ch]),
                deps=[vt_free[sl], sp_done[0]], dma=True, exempt=True,
                dma_sem=f"d_lv{sl}")
            for g4 in range(4):
                hs = list(range(g4 * 4, g4 * 4 + 4))
                cols = slice(g4 * 512, (g4 + 1) * 512)
                pi = g4 % 2
                smm = P.step("tensor", lambda eng, ch=ch, hs=hs, pi=pi: [eng.matmul(
                    psc[pi][:, i * 128:(i + 1) * 128],
                    k_fm[:, h, ch * CH:(ch + 1) * CH],
                    q_fm[:, h, ch * CH:(ch + 1) * CH],
                    start=True, stop=True)
                    for i, h in enumerate(hs)],
                    deps=[psc_free[pi], kf_tok[0]] + [prod[("q", h)] for h in hs])
                sev = P.step("vector", lambda eng, g4=g4, cols=cols, pi=pi:
                             eng.tensor_tensor(
                                 scT[g4 % 2][:], psc[pi][:], maskT[:, cols], ALU.mult),
                             deps=[smm, init_vec[-1], scT_free[g4 % 2]])
                psc_free[pi] = sev
                umm = P.step("tensor", lambda eng, hs=hs, sl=sl: [eng.matmul(
                    pupd[:, i * 128:(i + 1) * 128],
                    ksc2[sl][:, h * 128:(h + 1) * 128],
                    v_fm[:, 2 + sl, h * 128:(h + 1) * 128],
                    start=True, stop=True)
                    for i, h in enumerate(hs)],
                    deps=[lks, lv, pupd_free[0]])
                ksc_free[sl] = umm
                vt_free[sl] = umm
                uev = P.step("vector", lambda eng, hs=hs: [eng.scalar_tensor_tensor(
                    S_m[:, h * 128:(h + 1) * 128], S_m[:, h * 128:(h + 1) * 128],
                    cd_cols[:, h:h + 1], pupd[:, i * 128:(i + 1) * 128],
                    ALU.mult, ALU.add)
                    for i, h in enumerate(hs)][-1], deps=[umm, s3, zs])
                pupd_free[0] = uev
                pp = po[g4 % 2]
                omm_a = P.step("tensor", lambda eng, g4=g4, hs=hs, pp=pp, sl=sl: [
                    eng.matmul(pp[0][:, i * 128:(i + 1) * 128],
                               scT[g4 % 2][:, i * 128:(i + 1) * 128],
                               v_fm[:, 2 + sl, h * 128:(h + 1) * 128],
                               start=True, stop=True)
                    for i, h in enumerate(hs)],
                    deps=[sev, lv, po_free[g4 % 2]])
                scT_free[g4 % 2] = omm_a
                vt_free[sl] = omm_a
                omm = P.step("tensor", lambda eng, g4=g4, ch=ch, hs=hs, pp=pp: [
                    eng.matmul(pp[1][:, i * 128:(i + 1) * 128],
                               q_fm[:, h, ch * CH:(ch + 1) * CH],
                               S_bf[:, h * 128:(h + 1) * 128],
                               start=True, stop=True)
                    for i, h in enumerate(hs)],
                    deps=[sbf_tok[g4], po_free[g4 % 2]] + [prod[("q", h)] for h in hs])
                sb_guard[g4] = omm
                otm_ = P.step("scalar", lambda eng, hs=hs, pp=pp, g4=g4: [
                    eng.activation(o_tmp[g4 % 2][:, i * 128:(i + 1) * 128],
                                   pp[1][:, i * 128:(i + 1) * 128],
                                   AF.Identity, scale=qd_cols[:, h:h + 1])
                    for i, h in enumerate(hs)][-1],
                    deps=[omm, t3, otmp_free[g4 % 2]])
                oev = P.step("vector", lambda eng, ch=ch, cols=cols, pp=pp, g4=g4:
                             eng.tensor_tensor(o_tm[:, ch, cols], pp[0][:],
                                               o_tmp[g4 % 2][:], ALU.add),
                             deps=[omm_a, omm, otm_])
                po_free[g4 % 2] = oev
                otmp_free[g4 % 2] = oev
                sb_ = P.step("scalar", lambda eng, cols=cols: eng.activation(
                    S_bf[:, cols], S_m[:, cols], AF.Identity),
                    deps=[uev, sb_guard[g4]])
                sbf_tok[g4] = sb_
            scan_kf_last[0] = smm
            o_done[ch] = oev

    def emit_phase1(kst):
        nonlocal pacc
        row0 = kst * ST
        par = kst % 2
        is_sl = kst >= NLST
        sp = kst - NLST
        P.new_phase()
        with nc.psum_tensor(pnm("p1trA"), [128, 1024], BF16) as pA, \
             nc.psum_tensor(pnm("p1trB"), [128, 1024], BF16) as pB:
            ptr_l = [pA, pB]
            ptrl_free = [None, None]
            emit_ln_apply(row0, par, q="sync", ptrx=ptr_l, ptrx_free=ptrl_free)
            if is_sl:
                sxn = P.step("scalar", lambda eng, sp=sp: eng.dma_start(
                    out=sp_xn[sp], in_=xn_fm[:].rearrange("p a b -> p (a b)")),
                    deps=xn_done, dma=True, dma_sem="d_spxn")
                sp_last["xn"] = sxn
                xn_w_dep[0] = sxn
            P.flush()
        with ExitStack() as pctx:
            pv = [pctx.enter_context(nc.psum_tensor(pnm(f"pv{i}"), [128, 512], F32))
                  for i in range(8)]
            pv_free = [None, None]
            vev_tok = [None] * NCH
            for half in range(2):
                mm = None
                for k in range(KC):
                    s, slot, lt = ring_b.load(cache["v"][k], dep=cache_tok[("v", k)])
                    mm = P.step("tensor", lambda eng, slot=slot, k=k, half=half: [
                        eng.matmul(pv[ts2 * 4 + fo4][:],
                                   xn_fm[:, k, (half * 2 + ts2) * CH:(half * 2 + ts2 + 1) * CH],
                                   slot[:, fo4 * 512:(fo4 + 1) * 512],
                                   start=(k == 0), stop=(k == KC - 1),
                                   skip_group_check=True)
                        for ts2 in range(2) for fo4 in range(4)],
                        deps=[lt, pv_free[half % 2]] + xn_done)
                    ring_b.consumed(s, mm)
                evs = []
                for ts2 in range(2):
                    ev = P.step("scalar", lambda eng, half=half, ts2=ts2: [
                        eng.activation(v_fm[:, half * 2 + ts2, fo4 * 512:(fo4 + 1) * 512],
                                       pv[ts2 * 4 + fo4][:], AF.Silu)
                        for fo4 in range(4)][-1],
                        deps=[mm, spv_prev[half * 2 + ts2]])
                    vev_tok[half * 2 + ts2] = ev
                    evs.append(ev)
                pv_free[(half + 1) % 2] = evs[-1]
            if is_sl:
                for ch in range(NCH):
                    sv = P.step("scalar", lambda eng, sp=sp, ch=ch: eng.dma_start(
                        out=sp_v[sp, ch], in_=v_fm[:, ch, :]),
                        deps=[vev_tok[ch]], dma=True, dma_sem=f"d_spv{ch}")
                    sp_last["v"] = sv
                    spv_prev[ch] = sv
            P.flush()
        P.new_phase()
        with nc.psum_tensor(pnm("pacc0"), [128, ST], F32) as p0, \
             nc.psum_tensor(pnm("pacc1"), [128, ST], F32) as p1, \
             nc.psum_tensor(pnm("pacc2"), [128, ST], F32) as p2, \
             nc.psum_tensor(pnm("pacc3"), [128, ST], F32) as p3, \
             nc.psum_tensor(pnm("p1kA"), [128, 1024], BF16) as kA, \
             nc.psum_tensor(pnm("p1kB"), [128, 1024], BF16) as kB, \
             nc.psum_tensor(pnm("pupd1"), [128, 512], F32) as pu_:
            pacc = [p0, p1, p2, p3]
            ptrk = [kA, kB]
            ptrk_free = [None, None]
            pupd_free_l = [None]
            emit_qkv(["k"], rng=ring)
            xn_rd[0] = prod[("k", KC - 1)]
            if is_sl:
                skf = P.step("scalar", lambda eng, sp=sp: eng.dma_start(
                    out=sp_kf[sp], in_=k_fm[:].rearrange("p a b -> p (a b)")),
                    deps=[prod[("k", fo)] for fo in range(KC)],
                    dma=True, dma_sem="d_spkf")
                sp_last["kf"] = skf
                kf_w_dep[0] = skf
            for ch in range(NCH):
                sl = ch % 2
                kevs = [None, None]
                for hf in range(2):
                    hs = range(hf * 8, (hf + 1) * 8)
                    tt = P.step("tensor", lambda eng, hf=hf, ch=ch: [eng.transpose(
                        ptrk[hf][:, i * 128:(i + 1) * 128],
                        k_fm[:, hf * 8 + i, ch * CH:(ch + 1) * CH], ident_bf[:])
                        for i in range(8)],
                        deps=[ptrk_free[hf]] + [prod[("k", h)] for h in hs])
                    kev = P.step("scalar", lambda eng, hf=hf, sl=sl: [eng.activation(
                        ksc2[sl][:, (hf * 8 + i) * 128:(hf * 8 + i + 1) * 128],
                        ptrk[hf][:, i * 128:(i + 1) * 128], AF.Identity,
                        scale=kd_cols[:, hf * 8 + i:hf * 8 + i + 1])
                        for i in range(8)][-1],
                        deps=[tt, s2, ksc_free[sl]])
                    kevs[hf] = kev
                    ptrk_free[hf] = kev
                last_cons = kevs[1]
                if not is_sl:
                    for g4 in range(4):
                        hs = list(range(g4 * 4, g4 * 4 + 4))
                        umm = P.step("tensor", lambda eng, hs=hs, sl=sl, ch=ch: [
                            eng.matmul(pu_[:, i * 128:(i + 1) * 128],
                                       ksc2[sl][:, h * 128:(h + 1) * 128],
                                       v_fm[:, ch, h * 128:(h + 1) * 128],
                                       start=True, stop=True)
                            for i, h in enumerate(hs)],
                            deps=[kevs[g4 // 2], vev_tok[ch], pupd_free_l[0]])
                        uev = P.step("vector", lambda eng, hs=hs: [
                            eng.scalar_tensor_tensor(
                                S_m[:, h * 128:(h + 1) * 128],
                                S_m[:, h * 128:(h + 1) * 128],
                                cd_cols[:, h:h + 1], pu_[:, i * 128:(i + 1) * 128],
                                ALU.mult, ALU.add)
                            for i, h in enumerate(hs)][-1], deps=[umm, s3, zs])
                        pupd_free_l[0] = uev
                    last_cons = umm
                ksc_free[sl] = last_cons
                scan_kf_last[0] = kevs[1]
                if is_sl:
                    sks = P.step("scalar", lambda eng, sp=sp, ch=ch, sl=sl: eng.dma_start(
                        out=sp_ks[sp, ch], in_=ksc2[sl][:]),
                        deps=[kevs[1]], dma=True, dma_sem=f"d_spks{sl}")
                    sp_last["ks"] = sks
                    spks_prev[sl] = sks
                    ksc_free[sl] = [last_cons, sks]
            P.flush()

    def epi_gate_pieces(scan_q, epi):
        epi["ogt"] = []
        epi["pgt_free"] = [None]
        epi["sig_free"] = [None, None]

        def p0():
            g1mm = P.step("tensor", lambda eng: [eng.matmul(
                pg1[:], wg1_t[:, kc, :], xn_fm[:, kc, :],
                start=(kc == 0), stop=(kc == KC - 1)) for kc in range(KC)],
                deps=[wg1_tok] + xn_done + scan_q)
            epi["g1ev"] = P.step("scalar", lambda eng: eng.activation(
                g1_fm[:], pg1[:], AF.Identity), deps=[g1mm])
            epi["pgt_free"][0] = g1mm
            g1ev_out[0] = epi["g1ev"]

        pieces = [p0]
        for ts in range(NCH):
            for fo4 in range(4):
                def pf(ts=ts, fo4=fo4):
                    pa = fo4 % 2
                    mm = P.step("tensor", lambda eng, ts=ts, fo4=fo4: eng.matmul(
                        pgt[0][:], g1_fm[:, ts * 128:(ts + 1) * 128],
                        wg2_t[:, fo4 * 512:(fo4 + 1) * 512]),
                        deps=[epi["g1ev"], wg2_tok, epi["pgt_free"][0]] + scan_q)
                    th = P.step("scalar", lambda eng, pa=pa: eng.activation(
                        sig[pa][:], pgt[0][:], AF.Tanh, scale=0.5),
                        deps=[mm, epi["sig_free"][pa]])
                    epi["pgt_free"][0] = th
                    og_dst = og_ap(ts, fo4 * 512, (fo4 + 1) * 512)
                    ot = P.step("vector", lambda eng, ts=ts, fo4=fo4, pa=pa,
                                og_dst=og_dst: eng.scalar_tensor_tensor(
                                    og_dst, sig[pa][:], 1.0,
                                    o_tm[:, ts, fo4 * 512:(fo4 + 1) * 512],
                                    ALU.add, ALU.mult),
                                deps=[th, o_done[ts]])
                    epi["sig_free"][pa] = ot
                    if fo4 == 3:
                        sq_scr = out_hf[:].rearrange("p a b -> p (a b)")
                        sqv = P.step("vector", lambda eng, ts=ts, sq_scr=sq_scr:
                                     eng.tensor_tensor(sq_scr, og_ap(ts, 0, D),
                                                       og_ap(ts, 0, D), ALU.mult),
                                     deps=store_toks[-8:] + [ot])
                        mq = P.step("vector", lambda eng, ts=ts, sq_scr=sq_scr:
                                    eng.reduce_sum(msq[:, ts * H:(ts + 1) * H],
                                                   sq_scr.rearrange(
                                                       "p (h e) -> p h e", h=H),
                                                   AX), deps=[sqv])
                        epi["ogt"].append(mq)
                        epi["mq"] = mq
                pieces.append(pf)
        return pieces

    def emit_epi_norm(epi):
        sq = P.step("scalar", lambda eng: eng.activation(
            rstd_rms[:], msq[:], AF.Sqrt, scale=1.0 / HD,
            bias=eps_col[:, 0:1]), deps=epi["ogt"] + [zeps])
        rc = P.step("vector", lambda eng: eng.reciprocal(
            rstd_rms[:], rstd_rms[:]), deps=[sq])
        for ts in range(NCH):
            tm = tm_scr2[ts % 2]
            on = P.step("vector", lambda eng, ts=ts, tm=tm: [eng.tensor_scalar(
                tm[:, h * 128:(h + 1) * 128], og_ap(ts, h * 128, (h + 1) * 128),
                rstd_rms[:, ts * H + h:ts * H + h + 1], None, ALU.mult)
                for h in range(H)][-1],
                deps=[rc] + tm_free[ts % 2])
            tm_free[ts % 2] = []
            for hf in range(2):
                pb = (ts * 2 + hf) % 2
                tt = P.step("tensor", lambda eng, hf=hf, tm=tm, pb=pb: [eng.transpose(
                    ptrE[pb][:, i * 128:(i + 1) * 128],
                    tm[:, (hf * 8 + i) * 128:(hf * 8 + i + 1) * 128],
                    ident_bf[:]) for i in range(8)],
                    deps=[on, ptrE_free[pb]])
                tm_free[ts % 2].append(tt)
                ev = P.step("vector", lambda eng, ts=ts, hf=hf, pb=pb: [eng.tensor_scalar(
                    onf[:, hf * 8 + i, ts * CH:(ts + 1) * CH],
                    ptrE[pb][:, i * 128:(i + 1) * 128],
                    norm_cols[:, hf * 8 + i:hf * 8 + i + 1], None, ALU.mult)
                    for i in range(8)][-1], deps=[tt, nc_tok])
                ptrE_free[pb] = ev
            onf_done[ts] = ev
        dump_guard[0] = epi["mq"]
        pgt_free_out[0] = epi["pgt_free"][0]
        pgt_free_out[1] = epi["pgt_free"][0]

    def emit_ffn(gc0, gc1, fstate, og_guard=None, pieces=None):
        pge_free = fstate.setdefault("pge_free", [None, None])
        pue_free = fstate.setdefault("pue_free", [None, None])
        hg_free = fstate.setdefault("hg_free", [None, None])
        for gc in range(gc0, gc1):
            htoks = []
            for j in range(4):
                b = gc * 4 + j
                pa = b % 2
                sg_, slotg, ltg = ring.load(cache["gb"][b], dep=cache_tok[("gb", b)])
                mmg = P.step("tensor", lambda eng, slot=slotg, pa=pa: [eng.matmul(
                    pge[pa][:], slot[:, kc * 128:(kc + 1) * 128], xn_fm[:, kc, :],
                    start=(kc == 0), stop=(kc == KC - 1)) for kc in range(KC)],
                    deps=[ltg, pge_free[pa]] + xn_done)
                ring.consumed(sg_, mmg)
                su_, slotu, ltu = ring.load(cache["ub"][b], dep=cache_tok[("ub", b)])
                mmu = P.step("tensor", lambda eng, slot=slotu, pa=pa: [eng.matmul(
                    pue[pa][:], slot[:, kc * 128:(kc + 1) * 128], xn_fm[:, kc, :],
                    start=(kc == 0), stop=(kc == KC - 1)) for kc in range(KC)],
                    deps=[ltu, pue_free[pa]] + xn_done)
                ring.consumed(su_, mmu)
                fstate["last_mm"] = mmu
                hgt = P.step("scalar", lambda eng, pa=pa: eng.activation(
                    h_g[pa][:], pge[pa][:], AF.Silu), deps=[mmg, hg_free[pa]])
                pge_free[pa] = hgt
                ht = P.step("vector", lambda eng, b=b, pa=pa: eng.tensor_tensor(
                    hsp(b), h_g[pa][:], pue[pa][:], ALU.mult),
                    deps=[hgt, mmu, og_guard])
                pue_free[pa] = ht
                hg_free[pa] = ht
                htoks.append(ht)
                if pieces:
                    pieces.pop(0)()
            sq_t = P.step("vector", lambda eng, gc=gc: [eng.tensor_tensor(
                ssq[j][:], hsp(gc * 4 + j), hsp(gc * 4 + j), ALU.mult)
                for j in range(4)][-1], deps=htoks)
            ge_t = P.step("vector", lambda eng: [eng.tensor_tensor(
                ge[p][:], ssq[p[0]][:], ssq[p[1]][:], ALU.is_ge)
                for p in ge][-1], deps=[sq_t])
            wa = P.step("vector", lambda eng: [
                eng.tensor_tensor(ssq[0][:], ge[(0, 1)][:], ge[(0, 2)][:], ALU.add),
                eng.tensor_tensor(ssq[1][:], ge[(1, 2)][:], ge[(1, 3)][:], ALU.add),
                eng.tensor_tensor(ssq[2][:], ge[(2, 3)][:], ge[(0, 2)][:], ALU.subtract),
                eng.tensor_tensor(ssq[3][:], ge[(0, 3)][:], ge[(1, 3)][:], ALU.add),
            ], deps=[ge_t])
            wb = P.step("vector", lambda eng: [
                eng.tensor_tensor(ssq[0][:], ssq[0][:], ge[(0, 3)][:], ALU.add),
                eng.tensor_tensor(ssq[1][:], ssq[1][:], ge[(0, 1)][:], ALU.subtract),
                eng.tensor_tensor(ssq[2][:], ssq[2][:], ge[(1, 2)][:], ALU.subtract),
                eng.tensor_tensor(ssq[3][:], ssq[3][:], ge[(2, 3)][:], ALU.add),
            ], deps=[wa])
            w_t = P.step("vector", lambda eng: [
                eng.tensor_scalar(ssq[0][:], ssq[0][:], 1.5, None, ALU.is_ge),
                eng.tensor_scalar(ssq[1][:], ssq[1][:], 0.5, None, ALU.is_ge),
                eng.tensor_scalar(ssq[2][:], ssq[2][:], -0.5, None, ALU.is_ge),
                eng.tensor_scalar(ssq[3][:], ssq[3][:], 1.5, None, ALU.is_le),
            ], deps=[wb])
            ap_t = P.step("vector", lambda eng, gc=gc: [eng.tensor_tensor(
                hsp(gc * 4 + j), hsp(gc * 4 + j), ssq[j][:], ALU.mult)
                for j in range(4)][-1], deps=[w_t])
            for j in range(4):
                apply_tok[gc * 4 + j] = ap_t

    def emit_final(st, psq):
        nonlocal store_toks
        half_ev = []
        for half in range(2):
            c0 = half * 1024
            last_mm = None
            for dblk in range(KC):
                s, slot, lt = ring.load(cache["o"][dblk], dep=cache_tok[("o", dblk)])
                mm = P.step("tensor", lambda eng, slot=slot, dblk=dblk, c0=c0: [
                    eng.matmul(py[ts2 * 2 + fc][:],
                               onf[:, dblk, ts2 * 128:(ts2 + 1) * 128],
                               slot[:, c0 + fc * 512:c0 + (fc + 1) * 512],
                               start=(dblk == 0), stop=False,
                               skip_group_check=True)
                    for ts2 in range(NCH) for fc in range(2)],
                    deps=[lt] + onf_done + half_ev + psq)
                ring.consumed(s, mm)
                last_mm = mm
                wo_last[0] = mm
            for b in range(NB):
                s, slot, lt = ring.load(cache["db"][b], dep=cache_tok[("db", b)])
                mm = P.step("tensor", lambda eng, slot=slot, b=b, c0=c0: [
                    eng.matmul(py[ts2 * 2 + fc][:],
                               hsp(b)[:, ts2 * 128:(ts2 + 1) * 128],
                               slot[:, c0 + fc * 512:c0 + (fc + 1) * 512],
                               start=False, stop=(b == NB - 1),
                               skip_group_check=True)
                    for ts2 in range(NCH) for fc in range(2)],
                    deps=[lt, apply_tok[b]])
                ring.consumed(s, mm)
                last_mm = mm
            for ts in range(NCH):
                sl = ts % 2
                lx = P.step("gpsimd", lambda eng, ts=ts, c0=c0, sl=sl: eng.dma_start(
                    out=x_h2[sl][:],
                    in_=xw[cfg.LOOK + st * ST + ts * CH: cfg.LOOK + st * ST + (ts + 1) * CH,
                           c0:c0 + 1024]),
                    deps=x_free[sl], dma=True, exempt=True, dma_sem=f"d_x{sl}")
                x_free[sl] = []
                rv = P.step("vector", lambda eng, ts=ts, c0=c0, sl=sl: [
                    eng.tensor_tensor(out_hf[:, ts % 2, fc * 512:(fc + 1) * 512],
                                      py[ts * 2 + fc][:],
                                      x_h2[sl][:, fc * 512:(fc + 1) * 512],
                                      ALU.add) for fc in range(2)][-1],
                    deps=[last_mm, lx, dump_guard[0]] + out_free[ts % 2])
                out_free[ts % 2] = []
                x_free[sl] = [rv]
                half_ev.append(rv)
                so = P.step("scalar", lambda eng, ts=ts, c0=c0, st=st: eng.dma_start(
                    out=yout[st * ST + ts * CH: st * ST + (ts + 1) * CH, c0:c0 + 1024],
                    in_=out_hf[:, ts % 2, :]), deps=[rv], dma=True,
                    dma_sem=f"d_out{ts % 2}")
                out_free[ts % 2] = [so]
                store_toks.append(so)

    _pn = [0]

    def pnm(base):
        _pn[0] += 1
        return f"{base}_{_pn[0]}"

    NKST = NLST + NST
    emit_ln_stats(0, 0, q="sync")
    for kst in range(NKST):
        emit_phase1(kst)
        if kst + 1 < NKST:
            emit_ln_stats((kst + 1) * ST, (kst + 1) % 2, q="sync")
    sp_done[0] = ([sp_last["xn"], sp_last["kf"]] + list(spv_prev)
                  + list(spks_prev))
    P.new_phase()
    sbinit = P.step("scalar", lambda eng: eng.activation(
        S_bf[:], S_m[:], AF.Identity), deps=[])
    for g4 in range(4):
        sbf_tok[g4] = sbinit
    P.flush()

    for st in range(NST):
        P.new_phase()
        lxn = P.step("gpsimd", lambda eng, st=st: eng.dma_start(
            out=xn_fm[:].rearrange("p a b -> p (a b)"), in_=sp_xn[st]),
            deps=[xn_rd[0], sp_last["xn"]], dma=True, exempt=True,
            dma_sem="d_lxn")
        for ts in range(NCH):
            xn_done[ts] = lxn
        lkf = P.step("gpsimd", lambda eng, st=st: eng.dma_start(
            out=k_fm[:].rearrange("p a b -> p (a b)"), in_=sp_kf[st]),
            deps=[scan_kf_last[0], sp_last["kf"]], dma=True, exempt=True,
            dma_sem="d_lkf")
        kf_tok[0] = lkf
        with nc.psum_tensor(pnm("pacc0"), [128, ST], F32) as p0, \
             nc.psum_tensor(pnm("pacc1"), [128, ST], F32) as p1, \
             nc.psum_tensor(pnm("pacc2"), [128, ST], F32) as p2, \
             nc.psum_tensor(pnm("pacc3"), [128, ST], F32) as p3:
            pacc = [p0, p1, p2, p3]
            emit_qkv(["q"])
            P.flush()
        P.new_phase()
        with nc.psum_tensor(pnm("psc0"), [128, 512], F32) as ps0, \
             nc.psum_tensor(pnm("psc1"), [128, 512], F32) as ps1, \
             nc.psum_tensor(pnm("poA0"), [128, 512], F32) as poA0, \
             nc.psum_tensor(pnm("poA1"), [128, 512], F32) as poA1, \
             nc.psum_tensor(pnm("poB0"), [128, 512], F32) as poB0, \
             nc.psum_tensor(pnm("poB1"), [128, 512], F32) as poB1, \
             nc.psum_tensor(pnm("pupd"), [128, 512], F32) as pu_:
            psc, pupd = [ps0, ps1], pu_
            po = [[poA0, poA1], [poB0, poB1]]
            psc_free = [None, None]
            pupd_free = [None]
            po_free = [None, None]
            otmp_free = [None, None]
            scT_free = [None, None]
            emit_scan(st)
            P.flush()
        scan_q = P.snapshot()
        P.new_phase()
        with nc.psum_tensor(pnm("pge0"), [128, ST], F32) as g0, \
             nc.psum_tensor(pnm("pge1"), [128, ST], F32) as g1, \
             nc.psum_tensor(pnm("pue0"), [128, ST], F32) as u0, \
             nc.psum_tensor(pnm("pue1"), [128, ST], F32) as u1, \
             nc.psum_tensor(pnm("pg1"), [128, ST], F32) as pg1_, \
             nc.psum_tensor(pnm("pgt0"), [128, 512], F32) as pt0, \
             nc.psum_tensor(pnm("ptrE0"), [128, 1024], BF16) as pt1, \
             nc.psum_tensor(pnm("ptrE1"), [128, 1024], BF16) as pE:
            pge, pue = [g0, g1], [u0, u1]
            pg1, pgt, ptrE = pg1_, [pt0], [pt1, pE]
            ptrE_free = [None, None]
            fstate = {}
            epi = {}
            pieces = epi_gate_pieces(scan_q, epi)
            emit_ffn(0, KC // 2, fstate, pieces=pieces)
            for pc in pieces:
                pc()
            emit_epi_norm(epi)
            og_guard = onf_done[NCH - 1]
            emit_ffn(KC // 2, KC, fstate, og_guard=og_guard)
            xn_rd[0] = fstate["last_mm"]
            P.flush()
            psq = [fstate["pge_free"][0], fstate["pge_free"][1],
                   fstate["pue_free"][0], fstate["pue_free"][1],
                   pgt_free_out[0], g1ev_out[0], ptrE_free[0], ptrE_free[1]]
        with ExitStack() as pctx:
            py = [pctx.enter_context(nc.psum_tensor(pnm(f"py{i}"), [128, 512], F32))
                  for i in range(8)]
            emit_final(st, psq)
            P.flush()
        for _s in range(4):
            ring_b.free[_s].append(wo_last[0])

    P.new_phase()
    P.step("sync", lambda eng: eng.nop(), deps=list(store_toks))
    P.flush()

    ctx.close()
    return nc, dbg


def _prep_inputs(inputs, cfg, core_id):
    x = inputs["L0_attn_x"]
    b = core_id // 4
    s = core_id % 4
    t0 = s * cfg.SLICE
    sl = np.asarray(x[b, t0:t0 + cfg.SLICE], dtype=np.float32)
    if t0 >= cfg.LOOK:
        lb = np.asarray(x[b, t0 - cfg.LOOK:t0], dtype=np.float32)
    elif t0 == 0:
        lb = np.zeros((cfg.LOOK, D), dtype=np.float32)
    else:
        lb = np.concatenate([np.zeros((cfg.LOOK - t0, D), np.float32),
                             np.asarray(x[b, 0:t0], np.float32)], axis=0)
    return np.ascontiguousarray(np.concatenate([lb, sl], axis=0))


def _host_layouts(inputs, cfg):
    r = {}
    wq = np.asarray(inputs["L0_attn_Wq"], np.float32)
    wk = np.asarray(inputs["L0_attn_Wk"], np.float32)
    wv = np.asarray(inputs["L0_attn_Wv"], np.float32)
    wo = np.asarray(inputs["L0_attn_Wo"], np.float32)

    def fo_major(w):
        t = w.reshape(KC, 128, KC, 128)
        return np.ascontiguousarray(t.transpose(2, 1, 0, 3).reshape(KC, 128, D))

    r["wq_b"] = fo_major(wq)
    r["wk_b"] = fo_major(wk)
    r["wv_b"] = np.ascontiguousarray(wv.reshape(KC, 128, D))
    r["wo_b"] = np.ascontiguousarray(wo.reshape(KC, 128, D))
    r["wg1_in"] = np.ascontiguousarray(
        np.asarray(inputs["L0_attn_Wg1"], np.float32).reshape(KC, 128, HD))
    r["wg2_in"] = np.ascontiguousarray(np.asarray(inputs["L0_attn_Wg2"], np.float32))

    def member_major_cols(w):
        t = w.reshape(KC, 128, KC, 128, 4)
        return np.ascontiguousarray(t.transpose(2, 4, 1, 0, 3).reshape(NB, 128, D))

    r["wgb_in"] = member_major_cols(np.asarray(inputs["L0_ffn_w_gate"], np.float32))
    r["wub_in"] = member_major_cols(np.asarray(inputs["L0_ffn_w_up"], np.float32))
    wd = np.asarray(inputs["L0_ffn_w_down"], np.float32)
    r["wdb_in"] = np.ascontiguousarray(
        wd.reshape(KC, 128, 4, D).transpose(0, 2, 1, 3).reshape(NB, 128, D))
    r["normw_in"] = np.ascontiguousarray(
        np.asarray(inputs["L0_attn_norm_w"], np.float32).reshape(KC, 128).T)
    r["ld_in"] = np.ascontiguousarray(np.broadcast_to(
        np.asarray(inputs["L0_attn_log_decay"], np.float32)[None, :], (128, H)))

    idx = np.arange(128, dtype=np.float32)
    diff = idx[None, :] - idx[:, None]
    r["tmask_in"] = np.where(diff >= 0, diff, 1e5).astype(np.float32)
    r["cplus1_in"] = (idx + 1).reshape(128, 1).astype(np.float32)
    r["c127_in"] = (127.0 - idx).reshape(128, 1).astype(np.float32)
    r["ones_in"] = np.ones((128, 128), np.float32)
    r["ident_in"] = np.eye(128, dtype=np.float32)
    return r


def kernel(**inputs):
    cfg = Cfg()
    nc, _ = build_core_program(cfg)
    shared = _host_layouts(inputs, cfg)
    in_maps = []
    for c in range(8):
        m = dict(shared)
        m["xw"] = _prep_inputs(inputs, cfg, c)
        in_maps.append(m)
    res = run_bass_kernel_spmd(nc, in_maps, core_ids=list(range(8)))
    shards = [res.results[c]["yout"] for c in range(8)]
    out = np.empty((B, N, D), dtype=np.float32)
    for c in range(8):
        b, s = c // 4, c % 4
        out[b, s * cfg.SLICE:(s + 1) * cfg.SLICE] = shards[c]
    return out
